# version 01
# baseline (speedup 1.0000x reference)
import re
import numpy as np
from contextlib import ExitStack

import concourse.bass as bass
import concourse.mybir as mybir
from concourse.bass_utils import run_bass_kernel_spmd

F32 = mybir.dt.float32
BF16 = mybir.dt.bfloat16
I32 = mybir.dt.int32
AF = mybir.ActivationFunctionType
ALU = mybir.AluOpType
AX = mybir.AxisListType
ts = bass.ts

NCORES = 8
D = 1024
T = 2048
KC = 8
TTN = 4
TTW = 512
E = 8
QMAX_A = 32767.0
PI_2 = 1.5707963267948966

DEBUG = ()
STAGE = 5


class Ctx:

    PREFIX = re.compile(r"^(wA|wB|rbf2|rbf|wfF|evb|hb|xtt)\d+$")

    def __init__(self, nc, es):
        self.nc = nc
        self.es = es
        self.cnt = {}
        self.marks = {}
        self.gd = {}
        self.sems = {}
        self.pending = {"sync": [], "gpsimd": []}
        self.total_groups = set()

    def sem(self, es, name):
        s = es.enter_context(self.nc.semaphore(name))
        self.cnt[name] = 0
        self.sems[name] = s
        return s

    def gsem(self, group):
        name = "D_" + group
        if name not in self.sems:
            s = self.es.enter_context(self.nc.semaphore(name))
            self.sems[name] = s
            self.cnt[name] = 0
        return self.sems[name]

    def group_of(self, out_ap):
        n = out_ap.tensor.name
        m = self.PREFIX.match(n)
        return m.group(1) if m else n

    def dma(self, eng, sem_sentinel, out, in_):
        g = self.group_of(out)
        s = self.gsem(g)
        eng.dma_start(out=out, in_=in_).then_inc(s, 16)
        self.cnt[s.name] += 16
        q = "gpsimd" if sem_sentinel is self._gd_sentinel else "sync"
        self.pending[q].append((s, self.cnt[s.name]))
        return (s, self.cnt[s.name])

    def inc(self, inst, sem, n=1):
        inst.then_inc(sem, n)
        self.cnt[sem.name] += n
        return (sem, self.cnt[sem.name])

    def mark(self, name, sem):
        if sem is self._ld_sentinel or sem is self._gd_sentinel:
            q = "gpsimd" if sem is self._gd_sentinel else "sync"
            merged = {}
            for s, v in self.pending[q]:
                merged[s.name] = (s, max(v, merged.get(s.name, (s, 0))[1]))
            self.marks[name] = list(merged.values())
            self.pending[q] = []
        else:
            self.marks[name] = [(sem, self.cnt[sem.name])]

    def _resolve(self, dep):
        sem, v = dep
        if sem.name.startswith("D_") and                 sem.name[2:] in self.total_groups:
            v = self.cnt[sem.name]
        return sem, v

    def wait(self, eng, name):
        deps = self.marks[name]
        if isinstance(deps, tuple):
            deps = [deps]
        for dep in deps:
            sem, v = self._resolve(dep)
            if v > 0:
                eng.wait_ge(sem, v)

    def wdep(self, eng, dep):
        if dep is None:
            return
        if isinstance(dep, list):
            for d in dep:
                self.wdep(eng, d)
            return
        sem, v = self._resolve(dep)
        if v > 0:
            eng.wait_ge(sem, v)

    def wait_total(self, eng, group):
        s = self.gsem(group)
        if self.cnt[s.name] > 0:
            eng.wait_ge(s, self.cnt[s.name])

    def gw(self, eng, name):
        for d in self.gd.get(name, []):
            self.wdep(eng, d)

    def gsetw(self, name, dep):
        self.gd[name] = [dep]

    def gr(self, eng, name):
        lst = self.gd.get(name)
        if lst:
            self.wdep(eng, lst[0])

    def gaddr(self, name, dep):
        self.gd.setdefault(name, []).append(dep)


def build_program(debug=(), stage=STAGE):
    nc = bass.Bass("TRN2", target_bir_lowering=False, debug=True,
                   detect_race_conditions=False)

    EIN = lambda n, s, d=F32: nc.dram_tensor(n, s, d, kind="ExternalInput").ap()
    xin = EIN("x_fm", [KC, 128, T])
    cmask = EIN("cmask", [128, 192])
    onehot8 = EIN("onehot8", [8, 1024])
    w_qkvaT = EIN("w_qkvaT", [D, 3072])
    gamma_c = EIN("gamma_c", [128, KC])
    wup_n = EIN("wup_n", [3072, D])
    wgate_n = EIN("wgate_n", [3072, D])
    wdown_n = EIN("wdown_n", [D, 3072])
    w_toT = EIN("w_toT", [D, 512])
    w_fromT = EIN("w_fromT", [512, D])
    ldecay_c = EIN("ldecay_c", [128, 2])
    freq_c = EIN("freq_c", [128, 2])
    ln1g_c = EIN("ln1g_c", [128, KC])
    ln1b_c = EIN("ln1b_c", [128, KC])
    ln3g_c = EIN("ln3g_c", [128, KC])
    ln3b_c = EIN("ln3b_c", [128, KC])
    w_ffn1T = EIN("w_ffn1T", [D, 4096])
    b_ffn1_c = EIN("b_ffn1_c", [128, 32])
    w_ffn2T = EIN("w_ffn2T", [4096, D])
    b_ffn2_c = EIN("b_ffn2_c", [128, KC])
    wrT = EIN("wrT", [D, 8])
    w1T = EIN("w1T", [E, D, 4096])
    b1_c = EIN("b1_c", [E, 128, 32])
    w2T = EIN("w2T", [E, 4096, D])
    b2_c = EIN("b2_c", [E, 128, KC])

    out_fm = nc.dram_tensor("out_fm", [KC, 128, T], F32,
                            kind="ExternalOutput").ap()

    def dram(name, shape, dtype=F32):
        kind = "ExternalOutput" if name in debug else "Internal"
        return nc.dram_tensor(name, shape, dtype, kind=kind).ap()

    def blocked(name, nk=KC, dtype=F32):
        return dram(name, [nk, TTN, 128, TTW], dtype)

    xn_h = blocked("xn_h")
    q_h = blocked("q_h")
    kv_h = blocked("kv_h")
    a_h = blocked("a_h")
    s_h = blocked("s_h")
    up_h = blocked("up_h", 24)
    gate_h = blocked("gate_h", 24)
    g2_h = blocked("g2_h", 24)
    obf_h = blocked("obf_h", 24, BF16)
    x1_h = blocked("x1_h")
    xn1_h = blocked("xn1_h")
    xn1bf_h = blocked("xn1bf_h", KC, BF16)
    ln1bf_h = blocked("ln1bf_h", KC, BF16)
    x2_h = blocked("x2_h")
    ln3bf_h = blocked("ln3bf_h", KC, BF16)
    s1_h = blocked("s1_h")
    acc_h = blocked("acc_h")
    spo_h = dram("spo_h", [128, 4 * T]) if "spo_h" in debug else None
    wem_h = dram("wem_h", [8, T])
    lpdbg_h = dram("lpdbg_h", [24, 128, 1024], BF16) if "lpdbg_h" in debug \
        else None
    scdbg_h = dram("scdbg_h", [128, 32]) if "scdbg_h" in debug else None

    ccm_i = nc.dram_tensor("ccm_i", [1, 16], F32)
    ccm_o = nc.dram_tensor("ccm_o", [1, 16], F32)
    ccy_i = nc.dram_tensor("ccy_i", [128, 64], F32)
    ccy_o = nc.dram_tensor("ccy_o", [128, 64], F32)
    ccz_i = nc.dram_tensor("ccz_i", [128, 32], F32)
    ccz_o = nc.dram_tensor("ccz_o", [128, 32], F32)

    with ExitStack() as es0:
        ent = lambda t: es0.enter_context(t)
        ctx = Ctx(nc, es0)

        smN = [0]
        smT = ent(nc.sbuf_tensor("smT", [128, 896], F32))

        def sm(n):
            a = smT[:, smN[0]:smN[0] + n]
            smN[0] += n
            return a

        cm = sm(192)
        gam = sm(KC)
        l1g = sm(KC)
        l1b = sm(KC)
        l3g = sm(KC)
        l3b = sm(KC)
        bf2s = sm(KC)
        bf1 = sm(32)
        ldec = sm(2)
        freq = sm(2)
        b1e = sm(32)
        b2e = sm(KC)
        scol = sm(16)
        mx = sm(48)
        ystate = sm(KC)
        ybnd = sm(KC)
        _ycc_unused = sm(0)
        zstA = sm(4)
        zbnd = sm(4)
        _zcc_unused = sm(0)
        cfCD = sm(4)
        cfSD = sm(4)
        cfA = sm(4)
        cfB = sm(4)
        cfc = sm(2)
        cfs = sm(2)
        cfd = sm(2)
        wsc = sm(96)
        wrs = sm(96)
        wng = sm(96)
        rt32 = sm(32)
        cst = sm(2)
        onehe_a = sm(128)
        assert smN[0] <= 896, smN[0]
        smRT = ent(nc.sbuf_tensor("smRT", [1, 3 * TTW], F32))
        srow = smRT[0:1, 0:TTW]
        srow2 = smRT[0:1, TTW:2 * TTW]
        srow3 = smRT[0:1, 2 * TTW:3 * TTW]
        scal_t = ent(nc.sbuf_tensor("scal_t", [1, 32], F32))
        scal = scal_t[0:1, :]
        ycc = ent(nc.sbuf_tensor("ycc_t", [128, 64], F32))
        zcc = ent(nc.sbuf_tensor("zcc_t", [128, 32], F32))
        rt8_t = ent(nc.sbuf_tensor("rt8_t", [8, TTW], F32))
        idb = ent(nc.sbuf_tensor("idb", [128, 128], BF16))
        idf = ent(nc.sbuf_tensor("idf", [128, 128], F32))
        ones_k = ent(nc.sbuf_tensor("ones_k", [128, 1], F32))
        ones_b = ent(nc.sbuf_tensor("ones_b", [1, 128], F32))


        NEV = 10
        ev = [ent(nc.sbuf_tensor(f"ev{j}", [128, TTW], F32)) for j in range(NEV)]
        qi_t = ent(nc.sbuf_tensor("qi_t", [128, TTW], I32))
        wdq0 = ent(nc.sbuf_tensor("wdq0", [128, TTW], BF16))
        wdq1 = ent(nc.sbuf_tensor("wdq1", [128, TTW], BF16))

        wA = [ent(nc.sbuf_tensor(f"wA{k}", [128, 4096], BF16)) for k in range(KC)]
        wB = [ent(nc.sbuf_tensor(f"wB{k}", [128, 1024], BF16)) for k in range(32)]

        ps = [ent(nc.psum_tensor(f"ps{j}", [128, TTW], F32)) for j in range(4)]
        psS = ent(nc.psum_tensor("psS", [128, TTW], F32))
        psQ = ent(nc.psum_tensor("psQ", [128, TTW], F32))
        psB = ent(nc.psum_tensor("psB", [128, TTW], F32))
        psTb = ent(nc.psum_tensor("psTb", [128, 1024], BF16))

        sMM = ctx.sem(es0, "sMM")
        sDV = ctx.sem(es0, "sDV")
        sAC = ctx.sem(es0, "sAC")
        sGP = ctx.sem(es0, "sGP")
        sCC = ctx.sem(es0, "sCC")
        sLD = object()
        sGD = object()
        ctx._ld_sentinel = sLD
        ctx._gd_sentinel = sGD
        ctx.total_groups = {
            "xn_h", "q_h", "kv_h", "a_h", "s_h", "up_h", "gate_h", "g2_h",
            "obf_h", "x1_h", "xn1_h", "xn1bf_h", "ln1bf_h", "x2_h",
            "ln3bf_h", "s1_h", "acc_h", "wem_h", "out_fm", "spo_h"}

        blk = ent(nc.Block())

        def EVN(j):
            return f"ev{j}"

        def emit_quant(v, src_ap, out_ap, rs_ap, negs_ap, m_t, f_t):
            v.tensor_scalar(m_t[:], src_ap, rs_ap, None, ALU.mult)
            v.tensor_copy(qi_t[:], m_t[:])
            v.tensor_copy(f_t[:], qi_t[:])
            v.tensor_sub(m_t[:], m_t[:], f_t[:])
            v.scalar_tensor_tensor(f_t[:], m_t[:], 0.5, f_t[:],
                                   ALU.is_gt, ALU.add)
            v.scalar_tensor_tensor(m_t[:], m_t[:], -0.5, f_t[:],
                                   ALU.is_lt, ALU.subtract)
            v.tensor_scalar(out_ap, m_t[:], negs_ap, None, ALU.mult)

        def emit_collapse_max(j0, slot, tag, pre):
            @blk.tensor
            def _(te):
                ctx.wait(te, pre)
                ctx.gw(te, "psB")
                te.transpose(psB[0:1, 0:128], mx[:, j0:j0 + 1], idf[:])
                dep = ctx.inc(te.drain(), sMM)
                ctx.gsetw("psB", dep)
                ctx.mark(tag + "_c2", sMM)
            @blk.vector
            def _(v):
                ctx.wait(v, tag + "_c2")
                v.tensor_reduce(scal[0:1, slot:slot + 1], psB[0:1, 0:128],
                                AX.X, ALU.max)
                dep = ctx.inc(v.drain(), sDV)
                ctx.gaddr("psB", dep)
                ctx.mark(tag + "_c3", sDV)

        def emit_allreduce_scal(tag, pre):
            @blk.sync
            def _(sy):
                ctx.wait(sy, pre)
                ctx.dma(sy, sLD, ccm_i.ap(), scal[0:1, 0:16])
                ctx.mark(tag + "_in", sLD)
            @blk.gpsimd
            def _(g):
                ctx.wait(g, tag + "_in")
                ctx.inc(g.collective_compute(
                    "AllReduce", ALU.max,
                    replica_groups=[list(range(NCORES))],
                    ins=[ccm_i.ap().opt()], outs=[ccm_o.ap().opt()]), sCC)
                g.wait_ge(sCC, ctx.cnt["sCC"])
                ctx.inc(g.drain(), sGP)
                ctx.mark(tag + "_cc", sGP)
            @blk.sync
            def _(sy):
                ctx.wait(sy, tag + "_cc")
                ctx.dma(sy, sLD, scal[0:1, 0:16], ccm_o.ap())
                ctx.mark(tag + "_out", sLD)

        def emit_allreduce_add(tag, cc_i, cc_o, src, dst, pre):
            @blk.sync
            def _(sy):
                ctx.wait(sy, pre)
                ctx.dma(sy, sLD, cc_i.ap(), src)
                ctx.mark(tag + "_in", sLD)
            @blk.gpsimd
            def _(g):
                ctx.wait(g, tag + "_in")
                ctx.inc(g.collective_compute(
                    "AllReduce", ALU.add,
                    replica_groups=[list(range(NCORES))],
                    ins=[cc_i.ap().opt()], outs=[cc_o.ap().opt()]), sCC)
                g.wait_ge(sCC, ctx.cnt["sCC"])
                ctx.inc(g.drain(), sGP)
                ctx.mark(tag + "_cc", sGP)
            @blk.sync
            def _(sy):
                ctx.wait(sy, tag + "_cc")
                ctx.dma(sy, sLD, dst, cc_o.ap())
                ctx.mark(tag + "_out", sLD)

        def emit_scale_cols(slot, c0, tag, pre):
            @blk.vector
            def _(v):
                ctx.wait(v, pre)
                v.tensor_scalar(scal[0:1, slot:slot + 1],
                                scal[0:1, slot:slot + 1], 1.0 / QMAX_A, 1e-12,
                                ALU.mult, ALU.max)
                v.drain()
                v.reciprocal(scal[0:1, 24:25], scal[0:1, slot:slot + 1])
                v.drain()
                v.tensor_scalar(scal[0:1, 25:26], scal[0:1, slot:slot + 1],
                                -1.0, None, ALU.mult)
                ctx.inc(v.drain(), sDV)
                ctx.mark(tag + "_s", sDV)
            @blk.tensor
            def _(te):
                ctx.wait(te, tag + "_s")
                ctx.gw(te, "psB")
                te.matmul(psB[:, 0:2], ones_b[:], scal[0:1, 24:26],
                          start=True, stop=True)
                dep = ctx.inc(te.drain(), sMM)
                ctx.gsetw("psB", dep)
                ctx.mark(tag + "_b", sMM)
            @blk.vector
            def _(v):
                ctx.wait(v, tag + "_b")
                v.tensor_copy(scol[:, c0:c0 + 2], psB[:, 0:2])
                dep = ctx.inc(v.drain(), sDV)
                ctx.gaddr("psB", dep)
                ctx.mark(tag + "_cols", sDV)

        @blk.sync
        def _(sy):
            ctx.dma(sy, sLD, cm[:], cmask)
            ctx.dma(sy, sLD, gam[:], gamma_c)
            ctx.dma(sy, sLD, l1g[:], ln1g_c)
            ctx.dma(sy, sLD, l1b[:], ln1b_c)
            ctx.dma(sy, sLD, l3g[:], ln3g_c)
            ctx.dma(sy, sLD, l3b[:], ln3b_c)
            ctx.dma(sy, sLD, bf1[:], b_ffn1_c)
            ctx.dma(sy, sLD, bf2s[:], b_ffn2_c)
            ctx.dma(sy, sLD, ldec[:], ldecay_c)
            ctx.dma(sy, sLD, freq[:], freq_c)
            ctx.mark("p0_small", sLD)

        @blk.vector
        def _(v):
            v.memset(idb[:], 0.0)
            v.memset(idf[:], 0.0)
            ctx.inc(v.drain(), sDV)
            ctx.mark("p0_idz", sDV)

        @blk.gpsimd
        def _(g):
            ctx.wait(g, "p0_idz")
            g.affine_select(out=idb[:], in_=idb[:], compare_op=ALU.not_equal,
                            fill=1.0, base=0, pattern=[[-1, 128]],
                            channel_multiplier=1)
            g.affine_select(out=idf[:], in_=idf[:], compare_op=ALU.not_equal,
                            fill=1.0, base=0, pattern=[[-1, 128]],
                            channel_multiplier=1)
            ctx.inc(g.drain(), sGP)
            ctx.mark("p0_ident", sGP)
            for k in range(KC):
                ctx.dma(g, sGD, wA[k][:, 0:3072], w_qkvaT[ts(k, 128), :])
            ctx.mark("w_qkva", sGD)

        nc.const_aps.register(F32, 1e-5, cst[:, 0:1])
        nc.const_aps.register(F32, PI_2, cst[:, 1:2])

        @blk.vector
        def _(v):
            ctx.wait(v, "p0_small")
            v.memset(cst[:, 0:1], 1e-5)
            v.memset(cst[:, 1:2], PI_2)
            v.memset(ones_k[:], 1.0)
            v.memset(ones_b[:], 1.0)
            v.memset(ystate[:], 0.0)
            v.memset(zstA[:], 0.0)
            v.memset(mx[:], 0.0)
            v.memset(scal[:], 0.0)
            v.tensor_scalar(bf2s[:], bf2s[:], 0.1, None, ALU.mult)
            ctx.inc(v.drain(), sDV)
            ctx.mark("p0_ones", sDV)

        def emit_ln_tt(tag, src_dram_fn, pre_mark, want_cs, apply_k,
                       eps=1e-5):
            for k in range(KC):
                lj = k % 2
                sq = ev[8] if lj == 0 else ev[9]
                @blk.sync
                def _(sy, k=k, lj=lj):
                    if pre_mark:
                        ctx.wait(sy, pre_mark)
                    ctx.gw(sy, EVN(lj))
                    dep = ctx.dma(sy, sLD, ev[lj][:], src_dram_fn(k))
                    ctx.gsetw(EVN(lj), dep)
                    ctx.mark(tag + f"_ld_{k}", sLD)
                @blk.scalar
                def _(sc, k=k, lj=lj, sq=sq):
                    ctx.wait(sc, tag + f"_ld_{k}")
                    ctx.gw(sc, EVN(8 + lj))
                    sc.activation(sq[:], ev[lj][:], AF.Square)
                    dep = ctx.inc(sc.drain(), sAC)
                    ctx.gsetw(EVN(8 + lj), dep)
                    ctx.gaddr(EVN(lj), dep)
                    ctx.mark(tag + f"_sq_{k}", sAC)
                @blk.tensor
                def _(te, k=k, lj=lj, sq=sq):
                    ctx.wait(te, tag + f"_sq_{k}")
                    if k == 0:
                        ctx.gw(te, "psS")
                        ctx.gw(te, "psQ")
                    te.matmul(psS[0:1, :], ones_k[:], ev[lj][:],
                              start=(k == 0), stop=(k == KC - 1))
                    inst = te.matmul(psQ[0:1, :], ones_k[:], sq[:],
                                     start=(k == 0), stop=(k == KC - 1))
                    dep = ctx.inc(inst, sMM)
                    ctx.gaddr(EVN(lj), dep)
                    ctx.gaddr(EVN(8 + lj), dep)
                    if k == KC - 1:
                        ctx.gsetw("psS", dep)
                        ctx.gsetw("psQ", dep)
                        ctx.mark(tag + "_stats", sMM)

            @blk.vector
            def _(v):
                ctx.wait(v, tag + "_stats")
                v.tensor_scalar(srow[0:1, :], psS[0:1, :], 1.0 / D, None,
                                ALU.mult)
                v.tensor_scalar(srow2[0:1, :], psQ[0:1, :], 1.0 / D, None,
                                ALU.mult)
                v.tensor_mul(srow3[0:1, :], srow[0:1, :], srow[0:1, :])
                v.tensor_sub(srow2[0:1, :], srow2[0:1, :], srow3[0:1, :])
                v.tensor_scalar(srow2[0:1, :], srow2[0:1, :], 1e-37, None,
                                ALU.max)
                dep = ctx.inc(v.drain(), sDV)
                ctx.gaddr("psS", dep)
                ctx.gaddr("psQ", dep)
                ctx.mark(tag + "_var", sDV)

            @blk.scalar
            def _(sc):
                ctx.wait(sc, tag + "_var")
                if want_cs:
                    sc.activation(srow3[0:1, :], srow2[0:1, :], AF.Ln)
                    sc.activation(srow3[0:1, :], srow3[0:1, :], AF.Exp,
                                  scale=-0.5)
                sc.activation(srow2[0:1, :], srow2[0:1, :], AF.Ln, bias=eps)
                sc.activation(srow2[0:1, :], srow2[0:1, :], AF.Exp, scale=-0.5)
                ctx.inc(sc.drain(), sAC)
                ctx.mark(tag + "_istd", sAC)

            rows = [(srow, 5), (srow2, 6)] + ([(srow3, 7)] if want_cs else [])
            for ri, (rrow, evj) in enumerate(rows):
                @blk.tensor
                def _(te, ri=ri, rrow=rrow, evj=evj):
                    ctx.wait(te, tag + "_istd")
                    ctx.gw(te, "psB")
                    te.matmul(psB[:, :], ones_b[:], rrow[0:1, :],
                              start=True, stop=True)
                    dep = ctx.inc(te.drain(), sMM)
                    ctx.gsetw("psB", dep)
                    ctx.mark(tag + f"_bcm{ri}", sMM)
                @blk.vector
                def _(v, ri=ri, evj=evj):
                    ctx.wait(v, tag + f"_bcm{ri}")
                    ctx.gw(v, EVN(evj))
                    v.tensor_copy(ev[evj][:], psB[:, :])
                    dep = ctx.inc(v.drain(), sDV)
                    ctx.gsetw(EVN(evj), dep)
                    ctx.gaddr("psB", dep)
                    ctx.mark(tag + f"_bc{ri}", sDV)

            for k in range(KC):
                lj = 2 + (k % 2)
                @blk.sync
                def _(sy, k=k, lj=lj):
                    ctx.gw(sy, EVN(lj))
                    dep = ctx.dma(sy, sLD, ev[lj][:], src_dram_fn(k))
                    ctx.gsetw(EVN(lj), dep)
                    ctx.mark(tag + f"_ld2_{k}", sLD)
                @blk.vector
                def _(v, k=k, lj=lj):
                    ctx.wait(v, tag + f"_ld2_{k}")
                    if k == 0:
                        ctx.wait(v, tag + f"_bc{len(rows) - 1}")
                    ctx.gw(v, EVN(4))
                    v.tensor_sub(ev[4][:], ev[lj][:], ev[5][:])
                    dep = ctx.inc(v.drain(), sDV)
                    ctx.gsetw(EVN(4), dep)
                    ctx.gaddr(EVN(lj), dep)
                    ctx.mark(tag + f"_xc_{k}", sDV)
                apply_k(k)
            ctx.marks[tag + "_app"] = (sDV, ctx.cnt["sDV"])

        with ExitStack() as es1:
            hb = [es1.enter_context(nc.sbuf_tensor(f"hb{k}", [128, T], BF16))
                  for k in range(KC)]

            for tt in range(TTN):
                def l0_apply(k, tt=tt):
                    @blk.vector
                    def _(v, k=k, tt=tt):
                        ctx.wait(v, f"L0LN{tt}_xc_{k}")
                        dj = k % 2
                        ctx.gw(v, EVN(dj))
                        v.tensor_mul(ev[dj][:], ev[4][:], ev[6][:])
                        v.scalar_tensor_tensor(hb[k][:, ts(tt, TTW)], ev[4][:],
                                               gam[:, k:k + 1], ev[7][:],
                                               ALU.mult, ALU.mult)
                        v.tensor_reduce(mx[:, 0:1], ev[dj][:], AX.X, ALU.max,
                                        apply_absolute_value=True)
                        v.drain()
                        if tt == 0 and k == 0:
                            v.tensor_copy(mx[:, 1:2], mx[:, 0:1])
                        else:
                            v.tensor_max(mx[:, 1:2], mx[:, 1:2], mx[:, 0:1])
                        dep = ctx.inc(v.drain(), sDV)
                        ctx.gsetw(EVN(dj), dep)
                        ctx.gaddr(EVN(4), dep)
                        ctx.mark(f"l0ap_{tt}_{k}", sDV)
                    @blk.sync
                    def _(sy, k=k, tt=tt):
                        ctx.wait(sy, f"l0ap_{tt}_{k}")
                        dj = k % 2
                        dep = ctx.dma(sy, sLD, xn_h[k, tt], ev[dj][:])
                        ctx.gaddr(EVN(dj), dep)

                emit_ln_tt(f"L0LN{tt}", lambda k, tt=tt: xin[k][:, ts(tt, TTW)],
                           "p0_ones" if tt == 0 else None, True, l0_apply)

            @blk.vector
            def _(v):
                ctx.wait(v, f"l0ap_{TTN - 1}_{KC - 1}")
                ctx.inc(v.drain(), sDV)
                ctx.mark("xnmax_done", sDV)
            emit_collapse_max(1, 0, "axn", "xnmax_done")
            emit_allreduce_scal("axn_ar", "axn_c3")
            emit_scale_cols(0, 0, "sxn", "axn_ar_out")

            if stage >= 2:
                for tt in range(TTN):
                    for k in range(KC):
                        for role, mc in (("kv", 8 + k), ("a", 16 + k)):
                            bank = (tt * 16 + k * 2 + (0 if role == "kv" else 1)) % 4
                            @blk.tensor
                            def _(te, tt=tt, mc=mc, bank=bank):
                                ctx.wait(te, "w_qkva")
                                ctx.wait(te, f"L0LN{tt}_app")
                                ctx.gw(te, f"ps{bank}")
                                for kk in range(KC):
                                    inst = te.matmul(
                                        ps[bank][:, :],
                                        wA[kk][:, ts(mc, 128)],
                                        hb[kk][:, ts(tt, TTW)],
                                        start=(kk == 0), stop=(kk == KC - 1))
                                dep = ctx.inc(inst, sMM)
                                ctx.gsetw(f"ps{bank}", dep)
                                ctx.mark(f"qk_mm_{tt}_{mc}", sMM)
                            evj = 2 if role == "kv" else 3
                            @blk.scalar
                            def _(sc, tt=tt, mc=mc, bank=bank, evj=evj,
                                  role=role):
                                ctx.wait(sc, f"qk_mm_{tt}_{mc}")
                                ctx.gw(sc, EVN(evj))
                                if role == "kv":
                                    sc.copy(ev[evj][:], ps[bank][:, :])
                                else:
                                    sc.activation(ev[evj][:], ps[bank][:, :],
                                                  AF.Sigmoid)
                                dep = ctx.inc(sc.drain(), sAC)
                                ctx.gsetw(EVN(evj), dep)
                                ctx.gaddr(f"ps{bank}", dep)
                                ctx.mark(f"qk_ev_{tt}_{mc}", sAC)
                            @blk.sync
                            def _(sy, tt=tt, k=k, mc=mc, evj=evj, role=role):
                                ctx.wait(sy, f"qk_ev_{tt}_{mc}")
                                dsth = kv_h if role == "kv" else a_h
                                dep = ctx.dma(sy, sLD, dsth[k, tt], ev[evj][:])
                                ctx.gaddr(EVN(evj), dep)
                        @blk.vector
                        def _(v, tt=tt, k=k):
                            ctx.wait(v, f"qk_ev_{tt}_{16 + k}")
                            ctx.gw(v, EVN(5))
                            v.tensor_tensor_scan(ev[5][:], ev[3][:], ev[2][:],
                                                 ystate[:, k:k + 1],
                                                 ALU.mult, ALU.add)
                            v.drain()
                            v.tensor_copy(ystate[:, k:k + 1],
                                          ev[5][:, TTW - 1:TTW])
                            dep = ctx.inc(v.drain(), sDV)
                            ctx.gsetw(EVN(5), dep)
                            ctx.gaddr(EVN(2), dep)
                            ctx.gaddr(EVN(3), dep)
                            ctx.mark(f"scan1_{tt}_{k}", sDV)
                    for k in range(KC):
                        mc = k
                        bank = k % 4
                        @blk.tensor
                        def _(te, tt=tt, mc=mc, bank=bank):
                            ctx.gw(te, f"ps{bank}")
                            for kk in range(KC):
                                inst = te.matmul(
                                    ps[bank][:, :], wA[kk][:, ts(mc, 128)],
                                    hb[kk][:, ts(tt, TTW)],
                                    start=(kk == 0), stop=(kk == KC - 1))
                            dep = ctx.inc(inst, sMM)
                            ctx.gsetw(f"ps{bank}", dep)
                            ctx.mark(f"q_mm_{tt}_{mc}", sMM)
                        evj = 2 + (k % 2)
                        @blk.scalar
                        def _(sc, tt=tt, mc=mc, bank=bank, evj=evj):
                            ctx.wait(sc, f"q_mm_{tt}_{mc}")
                            ctx.gw(sc, EVN(evj))
                            sc.copy(ev[evj][:], ps[bank][:, :])
                            dep = ctx.inc(sc.drain(), sAC)
                            ctx.gsetw(EVN(evj), dep)
                            ctx.gaddr(f"ps{bank}", dep)
                            ctx.mark(f"q_ev_{tt}_{mc}", sAC)
                        @blk.sync
                        def _(sy, tt=tt, mc=mc, evj=evj):
                            ctx.wait(sy, f"q_ev_{tt}_{mc}")
                            dep = ctx.dma(sy, sLD, q_h[mc, tt], ev[evj][:])
                            ctx.gaddr(EVN(evj), dep)

                @blk.tensor
                def _(te):
                    ctx.inc(te.drain(), sMM)
                    ctx.mark("qkva_mm_done", sMM)

                @blk.vector
                def _(v):
                    ctx.wait(v, f"scan1_{TTN - 1}_{KC - 1}")
                    for c in range(KC):
                        v.tensor_scalar(ycc[:, ts(c, 8)], cm[:, ts(c, 8)],
                                        ystate[:, c:c + 1], None, ALU.mult)
                    ctx.inc(v.drain(), sDV)
                    ctx.mark("ycc_ready", sDV)
                emit_allreduce_add("yar", ccy_i, ccy_o, ycc[:], ycc[:],
                                   "ycc_ready")
                @blk.vector
                def _(v):
                    ctx.wait(v, "yar_out")
                    v.tensor_mul(ycc[:], ycc[:], cm[:, 64:128])
                    v.tensor_reduce(ybnd[:],
                                    ycc[:].rearrange("p (c s) -> p c s", s=8),
                                    AX.X, ALU.add)
                    ctx.inc(v.drain(), sDV)
                    ctx.mark("ybnd_ready", sDV)

            if stage >= 3:
                for tt in range(TTN):
                    for k in range(KC):
                        @blk.sync
                        def _(sy, tt=tt, k=k):
                            ctx.wait(sy, "sxn_cols")
                            if tt == 0 and k == 0:
                                ctx.wait_total(sy, "xn_h")
                            ctx.gw(sy, EVN(6))
                            dep = ctx.dma(sy, sLD, ev[6][:], xn_h[k, tt])
                            ctx.gsetw(EVN(6), dep)
                            ctx.mark(f"xqld_{tt}_{k}", sLD)
                        @blk.vector
                        def _(v, tt=tt, k=k):
                            ctx.wait(v, f"xqld_{tt}_{k}")
                            if tt == 0 and k == 0:
                                ctx.wait(v, "qkva_mm_done")
                            ctx.gw(v, EVN(8))
                            ctx.gw(v, EVN(9))
                            emit_quant(v, ev[6][:], hb[k][:, ts(tt, TTW)],
                                       scol[:, 0:1], scol[:, 1:2],
                                       ev[8], ev[9])
                            dep = ctx.inc(v.drain(), sDV)
                            for j in (6, 8, 9):
                                ctx.gsetw(EVN(j), dep)
                            ctx.mark(f"xq_{tt}_{k}", sDV)
                ctx.mark("xq_done", sDV)

                def emit_lpbq(tag, w_nat, O, I, dst_put, wfree_mark):
                    nblk = I // 32
                    for ot in range(O // 128):
                        for pc in range(I // TTW):
                            @blk.sync
                            def _(sy, ot=ot, pc=pc):
                                if wfree_mark and ot == 0 and pc == 0:
                                    pass
                                ctx.gw(sy, EVN(2))
                                dep = ctx.dma(sy, sLD, ev[2][:],
                                              w_nat[ts(ot, 128), ts(pc, TTW)])
                                ctx.gsetw(EVN(2), dep)
                                ctx.mark(tag + f"_ldn_{ot}_{pc}", sLD)
                            @blk.vector
                            def _(v, ot=ot, pc=pc):
                                ctx.wait(v, tag + f"_ldn_{ot}_{pc}")
                                nb = TTW // 32
                                v.tensor_reduce(
                                    wsc[:, pc * nb:(pc + 1) * nb],
                                    ev[2][:].rearrange("p (b i) -> p b i",
                                                       i=32),
                                    AX.X, ALU.max, apply_absolute_value=True)
                                dep = ctx.inc(v.drain(), sDV)
                                ctx.gaddr(EVN(2), dep)
                                ctx.mark(tag + f"_sc_{ot}_{pc}", sDV)
                        @blk.vector
                        def _(v, ot=ot):
                            ctx.wait(v, tag + f"_sc_{ot}_{I // TTW - 1}")
                            v.tensor_scalar(wsc[:, 0:nblk], wsc[:, 0:nblk],
                                            1.0 / 7.0, 1e-12, ALU.mult,
                                            ALU.max)
                            v.drain()
                            v.reciprocal(wrs[:, 0:nblk], wsc[:, 0:nblk])
                            v.tensor_scalar(wng[:, 0:nblk], wsc[:, 0:nblk],
                                            -1.0, None, ALU.mult)
                            ctx.inc(v.drain(), sDV)
                            ctx.mark(tag + f"_rs_{ot}", sDV)
                        for pc in range(I // TTW):
                            wdn = "wdq0" if (pc % 2 == 0) else "wdq1"
                            wd = wdq0 if (pc % 2 == 0) else wdq1
                            nb = TTW // 32
                            @blk.sync
                            def _(sy, ot=ot, pc=pc):
                                ctx.gw(sy, EVN(2))
                                dep = ctx.dma(sy, sLD, ev[2][:],
                                              w_nat[ts(ot, 128), ts(pc, TTW)])
                                ctx.gsetw(EVN(2), dep)
                                ctx.mark(tag + f"_ldn2_{ot}_{pc}", sLD)
                            @blk.vector
                            def _(v, ot=ot, pc=pc, wd=wd, wdn=wdn, nb=nb):
                                ctx.wait(v, tag + f"_ldn2_{ot}_{pc}")
                                ctx.wait(v, tag + f"_rs_{ot}")
                                ctx.gw(v, wdn)
                                ctx.gw(v, EVN(3))
                                ctx.gw(v, EVN(4))
                                rsv = wrs[:, pc * nb:(pc + 1) * nb].rearrange(
                                    "p (b o) -> p b o", o=1).broadcast_to(
                                    [128, nb, 32])
                                ngv = wng[:, pc * nb:(pc + 1) * nb].rearrange(
                                    "p (b o) -> p b o", o=1).broadcast_to(
                                    [128, nb, 32])
                                e2 = ev[2][:].rearrange("p (b i) -> p b i",
                                                        i=32)
                                e3 = ev[3][:].rearrange("p (b i) -> p b i",
                                                        i=32)
                                v.tensor_tensor(e3, e2, rsv, ALU.mult)
                                v.tensor_copy(qi_t[:], ev[3][:])
                                v.tensor_copy(ev[4][:], qi_t[:])
                                v.tensor_sub(ev[3][:], ev[3][:], ev[4][:])
                                v.scalar_tensor_tensor(ev[4][:], ev[3][:],
                                                       0.5, ev[4][:],
                                                       ALU.is_gt, ALU.add)
                                v.scalar_tensor_tensor(ev[3][:], ev[3][:],
                                                       -0.5, ev[4][:],
                                                       ALU.is_lt,
                                                       ALU.subtract)
                                e3b = ev[3][:].rearrange("p (b i) -> p b i",
                                                         i=32)
                                wdv = wd[:].rearrange("p (b i) -> p b i",
                                                      i=32)
                                v.tensor_tensor(wdv, e3b, ngv, ALU.mult)
                                dep = ctx.inc(v.drain(), sDV)
                                for nme in (EVN(2), EVN(3), EVN(4), wdn):
                                    ctx.gsetw(nme, dep)
                                ctx.mark(tag + f"_dq_{ot}_{pc}", sDV)
                            if lpdbg_h is not None and tag == "lpup":
                                @blk.sync
                                def _(sy, ot=ot, pc=pc, wd=wd):
                                    ctx.wait(sy, tag + f"_dq_{ot}_{pc}")
                                    ctx.dma(sy, sLD,
                                            lpdbg_h[ot, :, ts(pc, TTW)],
                                            wd[:])
                            @blk.tensor
                            def _(te, ot=ot, pc=pc, wd=wd, wdn=wdn):
                                ctx.wait(te, tag + f"_dq_{ot}_{pc}")
                                if wfree_mark:
                                    ctx.wait(te, wfree_mark)
                                ctx.gw(te, "psTb")
                                for jj in range(4):
                                    inst = te.transpose(
                                        psTb[:, ts(jj, 128)],
                                        wd[:, ts(jj, 128)], idb[:])
                                dep = ctx.inc(inst, sMM)
                                ctx.gsetw("psTb", dep)
                                ctx.gaddr(wdn, dep)
                                ctx.mark(tag + f"_tp_{ot}_{pc}", sMM)
                            @blk.scalar
                            def _(sc, ot=ot, pc=pc):
                                ctx.wait(sc, tag + f"_tp_{ot}_{pc}")
                                for jj in range(4):
                                    ic = pc * 4 + jj
                                    sc.copy(dst_put(ic, ot),
                                            psTb[:, ts(jj, 128)])
                                dep = ctx.inc(sc.drain(), sAC)
                                ctx.gaddr("psTb", dep)
                                ctx.mark(tag + f"_tpe_{ot}_{pc}", sAC)
                    ctx.mark(tag + "_done", sAC)

                def up_put(ic, ot):
                    return wA[ic][:, ts(ot, 128)]

                def gate_put(ic, ot):
                    return wB[ic * 3 + ot // 8][:, (ot % 8) * 128:
                                                (ot % 8) * 128 + 128]

                emit_lpbq("lpup", wup_n, 3072, D, up_put, "qkva_mm_done")
                emit_lpbq("lpgt", wgate_n, 3072, D, gate_put, None)

                for wi, (wtag, outh, mslot, wget) in enumerate((
                        ("up", up_h, 1,
                         lambda kk, mc: wA[kk][:, ts(mc, 128)]),
                        ("gt", gate_h, 2,
                         lambda kk, mc: wB[kk * 3 + mc // 8][:,
                                        (mc % 8) * 128:(mc % 8) * 128 + 128]))):
                    wmark = "lpup_done" if wtag == "up" else "lpgt_done"
                    mxj = 2 + wi
                    first = True
                    for tt in range(TTN):
                        for mc in range(24):
                            bank = (tt * 24 + mc) % 4
                            @blk.tensor
                            def _(te, tt=tt, mc=mc, bank=bank, wget=wget,
                                  wmark=wmark):
                                ctx.wait(te, wmark)
                                if tt == 0 and mc == 0:
                                    ctx.wait(te, "xq_done")
                                ctx.gw(te, f"ps{bank}")
                                for kk in range(KC):
                                    inst = te.matmul(
                                        ps[bank][:, :], wget(kk, mc),
                                        hb[kk][:, ts(tt, TTW)],
                                        start=(kk == 0), stop=(kk == KC - 1))
                                dep = ctx.inc(inst, sMM)
                                ctx.gsetw(f"ps{bank}", dep)
                                ctx.mark(f"{wtag}_mm_{tt}_{mc}", sMM)
                            evj = 2 + (mc % 2)
                            @blk.scalar
                            def _(sc, tt=tt, mc=mc, bank=bank, evj=evj,
                                  wtag=wtag):
                                ctx.wait(sc, f"{wtag}_mm_{tt}_{mc}")
                                ctx.gw(sc, EVN(evj))
                                sc.copy(ev[evj][:], ps[bank][:, :])
                                dep = ctx.inc(sc.drain(), sAC)
                                ctx.gsetw(EVN(evj), dep)
                                ctx.gaddr(f"ps{bank}", dep)
                                ctx.mark(f"{wtag}_ev_{tt}_{mc}", sAC)
                            @blk.vector
                            def _(v, tt=tt, mc=mc, evj=evj, wtag=wtag,
                                  mxj=mxj, first=first):
                                ctx.wait(v, f"{wtag}_ev_{tt}_{mc}")
                                v.tensor_reduce(mx[:, 0:1], ev[evj][:], AX.X,
                                                ALU.max,
                                                apply_absolute_value=True)
                                v.drain()
                                if first:
                                    v.tensor_copy(mx[:, mxj:mxj + 1],
                                                  mx[:, 0:1])
                                else:
                                    v.tensor_max(mx[:, mxj:mxj + 1],
                                                 mx[:, mxj:mxj + 1],
                                                 mx[:, 0:1])
                                dep = ctx.inc(v.drain(), sDV)
                                ctx.gaddr(EVN(evj), dep)
                                ctx.mark(f"{wtag}_mx_{tt}_{mc}", sDV)
                            first = False
                            @blk.sync
                            def _(sy, tt=tt, mc=mc, evj=evj, outh=outh,
                                  wtag=wtag):
                                ctx.wait(sy, f"{wtag}_mx_{tt}_{mc}")
                                dep = ctx.dma(sy, sLD, outh[mc, tt],
                                              ev[evj][:])
                                ctx.gaddr(EVN(evj), dep)
                                ctx.mark(f"{wtag}_sp_{tt}_{mc}", sLD)

                emit_collapse_max(2, 1, "aup", f"up_mx_{TTN - 1}_23")
                emit_collapse_max(3, 2, "agt", f"gt_mx_{TTN - 1}_23")
                emit_allreduce_scal("ug_ar", "agt_c3")
                emit_scale_cols(1, 2, "sup", "ug_ar_out")
                emit_scale_cols(2, 4, "sgt", "sup_cols")

                if scdbg_h is not None:
                    @blk.sync
                    def _(sy):
                        ctx.wait(sy, "sgt_cols")
                        ctx.dma(sy, sLD, scdbg_h[:, 0:16], scol[:, 0:16])
                        ctx.dma(sy, sLD, scdbg_h[0:1, 16:32], scal[0:1, 0:16])

                first = True
                for mc in range(24):
                    for tt in range(TTN):
                        @blk.sync
                        def _(sy, mc=mc, tt=tt):
                            if mc == 0 and tt == 0:
                                ctx.wait(sy, "sgt_cols")
                                ctx.wait_total(sy, "gate_h")
                            ctx.gw(sy, EVN(2))
                            dep = ctx.dma(sy, sLD, ev[2][:], gate_h[mc, tt])
                            ctx.gsetw(EVN(2), dep)
                            ctx.mark(f"g2ld_{mc}_{tt}", sLD)
                        @blk.vector
                        def _(v, mc=mc, tt=tt):
                            ctx.wait(v, f"g2ld_{mc}_{tt}")
                            for j in (3, 8, 9):
                                ctx.gw(v, EVN(j))
                            emit_quant(v, ev[2][:], ev[3][:], scol[:, 4:5],
                                       scol[:, 5:6], ev[8], ev[9])
                            dep = ctx.inc(v.drain(), sDV)
                            for j in (2, 3, 8, 9):
                                ctx.gsetw(EVN(j), dep) if j != 2 else \
                                    ctx.gaddr(EVN(2), dep)
                            ctx.mark(f"gdq_{mc}_{tt}", sDV)
                        @blk.scalar
                        def _(sc, mc=mc, tt=tt):
                            ctx.wait(sc, f"gdq_{mc}_{tt}")
                            ctx.gw(sc, EVN(6))
                            sc.activation(ev[6][:], ev[3][:], AF.Sigmoid)
                            dep = ctx.inc(sc.drain(), sAC)
                            ctx.gsetw(EVN(6), dep)
                            ctx.gaddr(EVN(3), dep)
                            ctx.mark(f"sig_{mc}_{tt}", sAC)
                        @blk.vector
                        def _(v, mc=mc, tt=tt, first=first):
                            ctx.wait(v, f"sig_{mc}_{tt}")
                            for j in (8, 9):
                                ctx.gw(v, EVN(j))
                            v.tensor_scalar(ev[8][:], ev[6][:], 65536.0, None,
                                            ALU.mult)
                            v.tensor_copy(qi_t[:], ev[8][:])
                            v.tensor_copy(ev[9][:], qi_t[:])
                            v.tensor_sub(ev[8][:], ev[8][:], ev[9][:])
                            v.scalar_tensor_tensor(ev[9][:], ev[8][:], 0.5,
                                                   ev[9][:], ALU.is_gt,
                                                   ALU.add)
                            v.scalar_tensor_tensor(ev[9][:], ev[8][:], -0.5,
                                                   ev[9][:], ALU.is_lt,
                                                   ALU.subtract)
                            v.tensor_scalar(ev[9][:], ev[9][:], -65535.0,
                                            -1.0 / 65536.0, ALU.max, ALU.mult)
                            v.tensor_mul(ev[6][:], ev[3][:], ev[9][:])
                            v.tensor_reduce(mx[:, 0:1], ev[6][:], AX.X,
                                            ALU.max,
                                            apply_absolute_value=True)
                            v.drain()
                            if first:
                                v.tensor_copy(mx[:, 4:5], mx[:, 0:1])
                            else:
                                v.tensor_max(mx[:, 4:5], mx[:, 4:5],
                                             mx[:, 0:1])
                            dep = ctx.inc(v.drain(), sDV)
                            for j in (6, 8, 9):
                                ctx.gsetw(EVN(j), dep)
                            ctx.gaddr(EVN(3), dep)
                            ctx.mark(f"g2_{mc}_{tt}", sDV)
                        first = False
                        @blk.sync
                        def _(sy, mc=mc, tt=tt):
                            ctx.wait(sy, f"g2_{mc}_{tt}")
                            dep = ctx.dma(sy, sLD, g2_h[mc, tt], ev[6][:])
                            ctx.gaddr(EVN(6), dep)
                            ctx.mark(f"g2sp_{mc}_{tt}", sLD)
                emit_collapse_max(4, 3, "ag2", f"g2_{23}_{TTN - 1}")
                emit_allreduce_scal("g2_ar", "ag2_c3")
                emit_scale_cols(3, 6, "sg2", "g2_ar_out")

                first = True
                for mc in range(24):
                    for tt in range(TTN):
                        @blk.sync
                        def _(sy, mc=mc, tt=tt):
                            if mc == 0 and tt == 0:
                                ctx.wait(sy, "sg2_cols")
                                ctx.wait_total(sy, "g2_h")
                                ctx.wait_total(sy, "up_h")
                            ctx.gw(sy, EVN(2))
                            dep = ctx.dma(sy, sLD, ev[2][:], g2_h[mc, tt])
                            ctx.gsetw(EVN(2), dep)
                            ctx.gw(sy, EVN(5))
                            dep = ctx.dma(sy, sLD, ev[5][:], up_h[mc, tt])
                            ctx.gsetw(EVN(5), dep)
                            ctx.mark(f"old_{mc}_{tt}", sLD)
                        @blk.vector
                        def _(v, mc=mc, tt=tt, first=first):
                            ctx.wait(v, f"old_{mc}_{tt}")
                            for j in (3, 6, 8, 9):
                                ctx.gw(v, EVN(j))
                            emit_quant(v, ev[2][:], ev[3][:], scol[:, 6:7],
                                       scol[:, 7:8], ev[8], ev[9])
                            emit_quant(v, ev[5][:], ev[6][:], scol[:, 2:3],
                                       scol[:, 3:4], ev[8], ev[9])
                            v.tensor_mul(ev[3][:], ev[3][:], ev[6][:])
                            v.tensor_reduce(mx[:, 0:1], ev[3][:], AX.X,
                                            ALU.max,
                                            apply_absolute_value=True)
                            v.drain()
                            if first:
                                v.tensor_copy(mx[:, 5:6], mx[:, 0:1])
                            else:
                                v.tensor_max(mx[:, 5:6], mx[:, 5:6],
                                             mx[:, 0:1])
                            dep = ctx.inc(v.drain(), sDV)
                            for j in (3, 6, 8, 9):
                                ctx.gsetw(EVN(j), dep)
                            ctx.gaddr(EVN(2), dep)
                            ctx.gaddr(EVN(5), dep)
                            ctx.mark(f"oraw_{mc}_{tt}", sDV)
                        first = False
                        @blk.sync
                        def _(sy, mc=mc, tt=tt):
                            ctx.wait(sy, f"oraw_{mc}_{tt}")
                            dep = ctx.dma(sy, sLD, g2_h[mc, tt], ev[3][:])
                            ctx.gaddr(EVN(3), dep)
                            ctx.mark(f"orsp_{mc}_{tt}", sLD)
                emit_collapse_max(5, 4, "ao", f"oraw_{23}_{TTN - 1}")
                emit_allreduce_scal("o_ar", "ao_c3")
                emit_scale_cols(4, 8, "so", "o_ar_out")

                for mc in range(24):
                    for tt in range(TTN):
                        @blk.sync
                        def _(sy, mc=mc, tt=tt):
                            if mc == 0 and tt == 0:
                                ctx.wait(sy, "so_cols")
                                ctx.wait_total(sy, "g2_h")
                            ctx.gw(sy, EVN(2))
                            dep = ctx.dma(sy, sLD, ev[2][:], g2_h[mc, tt])
                            ctx.gsetw(EVN(2), dep)
                            ctx.mark(f"oqld_{mc}_{tt}", sLD)
                        @blk.vector
                        def _(v, mc=mc, tt=tt):
                            ctx.wait(v, f"oqld_{mc}_{tt}")
                            for j in (8, 9):
                                ctx.gw(v, EVN(j))
                            ctx.gw(v, "wdq0")
                            emit_quant(v, ev[2][:], wdq0[:], scol[:, 8:9],
                                       scol[:, 9:10], ev[8], ev[9])
                            dep = ctx.inc(v.drain(), sDV)
                            for j in (8, 9):
                                ctx.gsetw(EVN(j), dep)
                            ctx.gaddr(EVN(2), dep)
                            ctx.gsetw("wdq0", dep)
                            ctx.mark(f"oq_{mc}_{tt}", sDV)
                        @blk.sync
                        def _(sy, mc=mc, tt=tt):
                            ctx.wait(sy, f"oq_{mc}_{tt}")
                            dep = ctx.dma(sy, sLD, obf_h[mc, tt], wdq0[:])
                            ctx.gaddr("wdq0", dep)
                            ctx.mark(f"oqsp_{mc}_{tt}", sLD)

                def down_put(ic, ot):
                    return wB[ic][:, ts(ot, 128)]

                emit_lpbq("lpdn", wdown_n, D, 3072, down_put,
                          f"gt_mm_{TTN - 1}_23")

                for tt in range(TTN):
                    for k in range(KC):
                        @blk.sync
                        def _(sy, tt=tt, k=k):
                            if tt == 0 and k == 0:
                                ctx.wait(sy, "ybnd_ready")
                                ctx.wait_total(sy, "a_h")
                                ctx.wait_total(sy, "kv_h")
                                ctx.wait_total(sy, "q_h")
                            for evj, hsrc in ((3, a_h), (5, kv_h), (6, q_h)):
                                ctx.gw(sy, EVN(evj))
                                dep = ctx.dma(sy, sLD, ev[evj][:],
                                              hsrc[k, tt])
                                ctx.gsetw(EVN(evj), dep)
                            ctx.mark(f"s2ld_{tt}_{k}", sLD)
                        @blk.vector
                        def _(v, tt=tt, k=k):
                            ctx.wait(v, f"s2ld_{tt}_{k}")
                            ctx.gw(v, EVN(7))
                            init = ybnd[:, k:k + 1] if tt == 0 else \
                                ystate[:, k:k + 1]
                            v.tensor_tensor_scan(ev[7][:], ev[3][:],
                                                 ev[5][:], init,
                                                 ALU.mult, ALU.add)
                            v.drain()
                            v.tensor_copy(ystate[:, k:k + 1],
                                          ev[7][:, TTW - 1:TTW])
                            v.tensor_mul(ev[7][:], ev[7][:], ev[6][:])
                            dep = ctx.inc(v.drain(), sDV)
                            ctx.gsetw(EVN(7), dep)
                            for j in (3, 5, 6):
                                ctx.gaddr(EVN(j), dep)
                            ctx.mark(f"smix_{tt}_{k}", sDV)
                        @blk.sync
                        def _(sy, tt=tt, k=k):
                            ctx.wait(sy, f"smix_{tt}_{k}")
                            dep = ctx.dma(sy, sLD, s_h[k, tt], ev[7][:])
                            ctx.gaddr(EVN(7), dep)
                            ctx.mark(f"ssp_{tt}_{k}", sLD)

                for tt in range(TTN):
                    for d in range(KC):
                        bank = (tt * KC + d) % 4
                        for kk in range(24):
                            wdn = "wdq0" if kk % 2 == 0 else "wdq1"
                            wd = wdq0 if kk % 2 == 0 else wdq1
                            @blk.sync
                            def _(sy, tt=tt, d=d, kk=kk, wdn=wdn, wd=wd):
                                if tt == 0 and d == 0 and kk == 0:
                                    ctx.wait_total(sy, "obf_h")
                                ctx.gw(sy, wdn)
                                dep = ctx.dma(sy, sLD, wd[:], obf_h[kk, tt])
                                ctx.gsetw(wdn, dep)
                                ctx.mark(f"dnld_{tt}_{d}_{kk}", sLD)
                            @blk.tensor
                            def _(te, tt=tt, d=d, kk=kk, bank=bank, wdn=wdn,
                                  wd=wd):
                                ctx.wait(te, f"dnld_{tt}_{d}_{kk}")
                                if kk == 0:
                                    if tt == 0 and d == 0:
                                        ctx.wait(te, "lpdn_done")
                                    ctx.gw(te, f"ps{bank}")
                                inst = te.matmul(ps[bank][:, :],
                                                 wB[kk][:, ts(d, 128)],
                                                 wd[:], start=(kk == 0),
                                                 stop=(kk == 23))
                                dep = ctx.inc(inst, sMM)
                                ctx.gaddr(wdn, dep)
                                if kk == 23:
                                    ctx.gsetw(f"ps{bank}", dep)
                                    ctx.mark(f"dn_mm_{tt}_{d}", sMM)
                        @blk.scalar
                        def _(sc, tt=tt, d=d, bank=bank):
                            ctx.wait(sc, f"dn_mm_{tt}_{d}")
                            ctx.gw(sc, EVN(4))
                            sc.activation(ev[4][:], ps[bank][:, :], AF.Silu)
                            dep = ctx.inc(sc.drain(), sAC)
                            ctx.gsetw(EVN(4), dep)
                            ctx.gaddr(f"ps{bank}", dep)
                            ctx.mark(f"dn_ev_{tt}_{d}", sAC)
                        @blk.sync
                        def _(sy, tt=tt, d=d):
                            ctx.wait(sy, f"ssp_{tt}_{d}")
                            ctx.gw(sy, EVN(6))
                            dep = ctx.dma(sy, sLD, ev[6][:], s_h[d, tt])
                            ctx.gsetw(EVN(6), dep)
                            ctx.gw(sy, EVN(7))
                            dep = ctx.dma(sy, sLD, ev[7][:],
                                          xin[d][:, ts(tt, TTW)])
                            ctx.gsetw(EVN(7), dep)
                            ctx.mark(f"x1ld_{tt}_{d}", sLD)
                        @blk.vector
                        def _(v, tt=tt, d=d):
                            ctx.wait(v, f"dn_ev_{tt}_{d}")
                            ctx.wait(v, f"x1ld_{tt}_{d}")
                            v.tensor_mul(ev[4][:], ev[4][:], ev[6][:])
                            v.tensor_add(ev[4][:], ev[4][:], ev[7][:])
                            dep = ctx.inc(v.drain(), sDV)
                            ctx.gsetw(EVN(4), dep)
                            for j in (6, 7):
                                ctx.gaddr(EVN(j), dep)
                            ctx.mark(f"x1_{tt}_{d}", sDV)
                        @blk.sync
                        def _(sy, tt=tt, d=d):
                            ctx.wait(sy, f"x1_{tt}_{d}")
                            dep = ctx.dma(sy, sLD, x1_h[d, tt], ev[4][:])
                            ctx.gaddr(EVN(4), dep)
                            ctx.mark(f"x1sp_{tt}_{d}", sLD)


        if stage >= 4:
            for tt in range(TTN):
                def l1_apply(k, tt=tt):
                    @blk.vector
                    def _(v, k=k, tt=tt):
                        ctx.wait(v, f"L1LN{tt}_xc_{k}")
                        dj = k % 2
                        ctx.gw(v, EVN(dj))
                        ctx.gw(v, "wdq1")
                        v.tensor_mul(ev[dj][:], ev[4][:], ev[6][:])
                        v.tensor_copy(wdq1[:], ev[dj][:])
                        dep = ctx.inc(v.drain(), sDV)
                        ctx.gsetw(EVN(dj), dep)
                        ctx.gsetw("wdq1", dep)
                        ctx.gaddr(EVN(4), dep)
                        ctx.mark(f"l1ap_{tt}_{k}", sDV)
                    @blk.sync
                    def _(sy, k=k, tt=tt):
                        ctx.wait(sy, f"l1ap_{tt}_{k}")
                        dj = k % 2
                        dep = ctx.dma(sy, sLD, xn1_h[k, tt], ev[dj][:])
                        ctx.gaddr(EVN(dj), dep)
                        dep = ctx.dma(sy, sLD, xn1bf_h[k, tt], wdq1[:])
                        ctx.gaddr("wdq1", dep)
                        ctx.mark(f"xn1sp_{tt}_{k}", sLD)

                emit_ln_tt(f"L1LN{tt}",
                           lambda k, tt=tt: x1_h[k, tt],
                           f"x1sp_{tt}_{KC - 1}" if stage >= 3 else None,
                           False, l1_apply)

            for tt in range(TTN):
                def ln1_apply(k, tt=tt):
                    @blk.vector
                    def _(v, k=k, tt=tt):
                        ctx.wait(v, f"LN1{tt}_xc_{k}")
                        dj = k % 2
                        ctx.gw(v, EVN(dj))
                        ctx.gw(v, "wdq1")
                        v.tensor_mul(ev[dj][:], ev[4][:], ev[6][:])
                        v.tensor_scalar(wdq1[:], ev[dj][:], l1g[:, k:k + 1],
                                        l1b[:, k:k + 1], ALU.mult, ALU.add)
                        dep = ctx.inc(v.drain(), sDV)
                        ctx.gsetw(EVN(dj), dep)
                        ctx.gsetw("wdq1", dep)
                        ctx.gaddr(EVN(4), dep)
                        ctx.mark(f"ln1ap_{tt}_{k}", sDV)
                    @blk.sync
                    def _(sy, k=k, tt=tt):
                        ctx.wait(sy, f"ln1ap_{tt}_{k}")
                        dep = ctx.dma(sy, sLD, ln1bf_h[k, tt], wdq1[:])
                        ctx.gaddr("wdq1", dep)
                        ctx.mark(f"ln1sp_{tt}_{k}", sLD)

                emit_ln_tt(f"LN1{tt}",
                           lambda k, tt=tt: xn1_h[k, tt],
                           f"xn1sp_{tt}_{KC - 1}", False, ln1_apply)

            with ExitStack() as es2:
                spec = es2.enter_context(
                    nc.sbuf_tensor("spec", [128, 4 * T], F32))
                rbf = [es2.enter_context(
                    nc.sbuf_tensor(f"rbf{j}", [128, TTW], BF16))
                    for j in range(KC)]
                wfF = [es2.enter_context(
                    nc.sbuf_tensor(f"wfF{j}", [128, 128], F32))
                    for j in range(4)]

                @blk.gpsimd
                def _(g):
                    if stage >= 3:
                        ctx.wait(g, f"up_mm_{TTN - 1}_23")
                    else:
                        ctx.wait(g, "qkva_mm_done")
                    for k in range(KC):
                        ctx.dma(g, sGD, wA[k][:, 0:512],
                                w_toT[ts(k, 128), :])
                    ctx.mark("w_to_from", sGD)



                for tt in range(TTN):
                    @blk.sync
                    def _(sy, tt=tt):
                        ctx.wait(sy, f"ln1sp_{tt}_{KC - 1}")
                        for k in range(KC):
                            ctx.gw(sy, f"rbf{k}")
                            dep = ctx.dma(sy, sLD, rbf[k][:],
                                          ln1bf_h[k, tt])
                            ctx.gsetw(f"rbf{k}", dep)
                        ctx.mark(f"wt_rhs_{tt}", sLD)
                    for mc in range(4):
                        bank = (tt * 4 + mc) % 4
                        @blk.tensor
                        def _(te, tt=tt, mc=mc, bank=bank):
                            ctx.wait(te, "w_to_from")
                            ctx.wait(te, f"wt_rhs_{tt}")
                            ctx.gw(te, f"ps{bank}")
                            for kk in range(KC):
                                inst = te.matmul(
                                    ps[bank][:, :],
                                    wA[kk][:, mc * 128:mc * 128 + 128],
                                    rbf[kk][:],
                                    start=(kk == 0), stop=(kk == KC - 1))
                            dep = ctx.inc(inst, sMM)
                            ctx.gsetw(f"ps{bank}", dep)
                            ctx.mark(f"wt_mm_{tt}_{mc}", sMM)
                        @blk.scalar
                        def _(sc, tt=tt, mc=mc, bank=bank):
                            ctx.wait(sc, f"wt_mm_{tt}_{mc}")
                            sc.activation(
                                spec[:, mc * T + tt * TTW:
                                     mc * T + tt * TTW + TTW],
                                ps[bank][:, :], AF.Identity, scale=0.1)
                            dep = ctx.inc(sc.drain(), sAC)
                            ctx.gaddr(f"ps{bank}", dep)
                            ctx.mark(f"wt_ev_{tt}_{mc}", sAC)

                @blk.scalar
                def _(sc):
                    ctx.wait(sc, "p0_ones")
                    sc.activation(cfd[:], ldec[:], AF.Sigmoid)
                    sc.activation(cfs[:], freq[:], AF.Tanh)
                    ctx.inc(sc.drain(), sAC)
                    ctx.mark("cf1", sAC)
                @blk.vector
                def _(v):
                    ctx.wait(v, "cf1")
                    v.tensor_scalar(cfs[:], cfs[:], 0.1, None, ALU.mult)
                    ctx.inc(v.drain(), sDV)
                    ctx.mark("cf2", sDV)
                @blk.scalar
                def _(sc):
                    ctx.wait(sc, "cf2")
                    sc.activation(cfc[:], cfs[:], AF.Sin, bias=PI_2)
                    sc.activation(cfs[:], cfs[:], AF.Sin)
                    ctx.inc(sc.drain(), sAC)
                    ctx.mark("cf3", sAC)
                @blk.vector
                def _(v):
                    ctx.wait(v, "cf3")
                    v.tensor_mul(cfCD[:, 0:2], cfc[:], cfd[:])
                    v.drain()
                    v.tensor_copy(cfCD[:, 2:4], cfCD[:, 0:2])
                    v.tensor_mul(cfSD[:, 2:4], cfs[:], cfd[:])
                    v.drain()
                    v.tensor_scalar(cfSD[:, 0:2], cfSD[:, 2:4], -1.0, None,
                                    ALU.mult)
                    ctx.inc(v.drain(), sDV)
                    ctx.mark("cf_done", sDV)

                for half in range(2):
                    for tt in range(TTN):
                        @blk.vector
                        def _(v, half=half, tt=tt):
                            if half == 0 and tt == 0:
                                ctx.wait(v, "cf_done")
                            ctx.wait(v, f"wt_ev_{tt}_{half}")
                            ctx.wait(v, f"wt_ev_{tt}_{2 + half}")
                            sl_r = half * T + tt * TTW
                            sl_i = (2 + half) * T + tt * TTW
                            for j in (0, 1, 2, 3):
                                ctx.gw(v, EVN(j))
                            v.tensor_scalar(ev[0][:], spec[:, sl_r:sl_r + TTW],
                                            cfc[:, half:half + 1], None,
                                            ALU.mult)
                            v.tensor_scalar(ev[1][:], spec[:, sl_i:sl_i + TTW],
                                            cfs[:, half:half + 1], None,
                                            ALU.mult)
                            v.tensor_scalar(ev[2][:], spec[:, sl_r:sl_r + TTW],
                                            cfs[:, half:half + 1], None,
                                            ALU.mult)
                            v.tensor_scalar(ev[3][:], spec[:, sl_i:sl_i + TTW],
                                            cfc[:, half:half + 1], None,
                                            ALU.mult)
                            v.tensor_sub(spec[:, sl_r:sl_r + TTW], ev[0][:],
                                         ev[1][:])
                            v.tensor_add(spec[:, sl_i:sl_i + TTW], ev[2][:],
                                         ev[3][:])
                            dep = ctx.inc(v.drain(), sDV)
                            for j in (0, 1, 2, 3):
                                ctx.gsetw(EVN(j), dep)
                            ctx.mark(f"vpre_{half}_{tt}", sDV)
                @blk.vector
                def _(v):
                    ctx.inc(v.drain(), sDV)
                    ctx.mark("vpre_done", sDV)

                spec3 = spec[:].rearrange("p (c t) -> p c t", c=4)

                def scan_step(v, zprev_ap, vslot_ap, dst_ap):
                    v.tensor_tensor(cfA[:], zprev_ap, cfCD[:], ALU.mult)
                    v.tensor_tensor(cfB[:, 0:2], zprev_ap[:, 2:4],
                                    cfSD[:, 0:2], ALU.mult)
                    v.tensor_tensor(cfB[:, 2:4], zprev_ap[:, 0:2],
                                    cfSD[:, 2:4], ALU.mult)
                    v.drain()
                    v.tensor_add(cfA[:], cfA[:], cfB[:])
                    v.drain()
                    v.tensor_tensor(cfA[:], cfA[:], vslot_ap, ALU.add)
                    v.drain()
                    v.tensor_scalar(dst_ap, cfA[:], 10.0, -10.0, ALU.min,
                                    ALU.max)
                    v.drain()

                @blk.vector
                def _(v):
                    ctx.wait(v, "vpre_done")
                    for t in range(T):
                        scan_step(v, zstA[:], spec3[:, :, t], zstA[:])
                    ctx.inc(v.drain(), sDV)
                    ctx.mark("scanA_done", sDV)

                @blk.vector
                def _(v):
                    ctx.wait(v, "scanA_done")
                    for c in range(4):
                        v.tensor_scalar(zcc[:, ts(c, 8)],
                                        cm[:, 128 + c * 8:128 + c * 8 + 8],
                                        zstA[:, c:c + 1], None, ALU.mult)
                    ctx.inc(v.drain(), sDV)
                    ctx.mark("zcc_ready", sDV)
                emit_allreduce_add("zar", ccz_i, ccz_o, zcc[:], zcc[:],
                                   "zcc_ready")
                @blk.vector
                def _(v):
                    ctx.wait(v, "zar_out")
                    v.tensor_mul(zcc[:], zcc[:], cm[:, 160:192])
                    v.tensor_reduce(zbnd[:],
                                    zcc[:].rearrange("p (c s) -> p c s", s=8),
                                    AX.X, ALU.add)
                    ctx.inc(v.drain(), sDV)
                    ctx.mark("zbnd_ready", sDV)

                @blk.vector
                def _(v):
                    ctx.wait(v, "zbnd_ready")
                    for t in range(T):
                        zp = zbnd[:] if t == 0 else spec3[:, :, t - 1]
                        scan_step(v, zp, spec3[:, :, t], spec3[:, :, t])
                    ctx.inc(v.drain(), sDV)
                    ctx.mark("scanB_done", sDV)

                if spo_h is not None:
                    @blk.sync
                    def _(sy):
                        ctx.wait(sy, "scanB_done")
                        ctx.dma(sy, sLD, spo_h, spec[:])
                        ctx.mark("spo_dbg", sLD)

                for d in range(KC):
                    @blk.sync
                    def _(sy, d=d):
                        for c in range(4):
                            ctx.gw(sy, f"wfF{c}")
                            dep = ctx.dma(sy, sLD, wfF[c][:],
                                          w_fromT[ts(c, 128),
                                                  d * 128:d * 128 + 128])
                            ctx.gsetw(f"wfF{c}", dep)
                        ctx.mark(f"wfld_{d}", sLD)
                    for tt in range(TTN):
                        bank = (d * TTN + tt) % 4
                        @blk.tensor
                        def _(te, tt=tt, d=d, bank=bank):
                            if tt == 0 and d == 0:
                                ctx.wait(te, "scanB_done")
                            ctx.wait(te, f"wfld_{d}")
                            ctx.gw(te, f"ps{bank}")
                            for c in range(4):
                                inst = te.matmul(
                                    ps[bank][:, :], wfF[c][:],
                                    spec[:, c * T + tt * TTW:
                                         c * T + tt * TTW + TTW],
                                    start=(c == 0), stop=(c == 3))
                            dep = ctx.inc(inst, sMM)
                            ctx.gsetw(f"ps{bank}", dep)
                            for c in range(4):
                                ctx.gaddr(f"wfF{c}", dep)
                            ctx.mark(f"wf_mm_{tt}_{d}", sMM)
                        @blk.scalar
                        def _(sc, tt=tt, d=d, bank=bank):
                            ctx.wait(sc, f"wf_mm_{tt}_{d}")
                            ctx.gw(sc, EVN(8))
                            sc.activation(ev[8][:], ps[bank][:, :],
                                          AF.Identity, scale=0.1)
                            dep = ctx.inc(sc.drain(), sAC)
                            ctx.gsetw(EVN(8), dep)
                            ctx.gaddr(f"ps{bank}", dep)
                            ctx.mark(f"wf_ev_{tt}_{d}", sAC)
                        @blk.sync
                        def _(sy, tt=tt, d=d):
                            if tt == 0 and d == 0:
                                ctx.wait_total(sy, "xn1_h")
                            ctx.gw(sy, EVN(9))
                            dep = ctx.dma(sy, sLD, ev[9][:], xn1_h[d, tt])
                            ctx.gsetw(EVN(9), dep)
                            ctx.mark(f"wf_x_{tt}_{d}", sLD)
                        @blk.vector
                        def _(v, tt=tt, d=d):
                            ctx.wait(v, f"wf_ev_{tt}_{d}")
                            ctx.wait(v, f"wf_x_{tt}_{d}")
                            v.tensor_add(ev[8][:], ev[8][:], ev[9][:])
                            dep = ctx.inc(v.drain(), sDV)
                            ctx.gsetw(EVN(8), dep)
                            ctx.gaddr(EVN(9), dep)
                            ctx.mark(f"x2_{tt}_{d}", sDV)
                        @blk.sync
                        def _(sy, tt=tt, d=d):
                            ctx.wait(sy, f"x2_{tt}_{d}")
                            dep = ctx.dma(sy, sLD, x2_h[d, tt], ev[8][:])
                            ctx.gaddr(EVN(8), dep)
                            ctx.mark(f"x2sp_{tt}_{d}", sLD)

            for tt in range(TTN):
                def ln3_apply(k, tt=tt):
                    @blk.vector
                    def _(v, k=k, tt=tt):
                        ctx.wait(v, f"LN3{tt}_xc_{k}")
                        dj = k % 2
                        ctx.gw(v, EVN(dj))
                        ctx.gw(v, "wdq1")
                        v.tensor_mul(ev[dj][:], ev[4][:], ev[6][:])
                        v.tensor_scalar(wdq1[:], ev[dj][:], l3g[:, k:k + 1],
                                        l3b[:, k:k + 1], ALU.mult, ALU.add)
                        dep = ctx.inc(v.drain(), sDV)
                        ctx.gsetw(EVN(dj), dep)
                        ctx.gsetw("wdq1", dep)
                        ctx.gaddr(EVN(4), dep)
                        ctx.mark(f"ln3ap_{tt}_{k}", sDV)
                    @blk.sync
                    def _(sy, k=k, tt=tt):
                        ctx.wait(sy, f"ln3ap_{tt}_{k}")
                        dep = ctx.dma(sy, sLD, ln3bf_h[k, tt], wdq1[:])
                        ctx.gaddr("wdq1", dep)
                        ctx.mark(f"ln3sp_{tt}_{k}", sLD)

                emit_ln_tt(f"LN3{tt}",
                           lambda k, tt=tt: x2_h[k, tt],
                           f"x2sp_{tt}_{KC - 1}", False, ln3_apply)

        if stage >= 5:
            with ExitStack() as es3:
                evb = [es3.enter_context(
                    nc.sbuf_tensor(f"evb{j}", [128, TTW], BF16))
                    for j in range(32)]
                rbf2 = [es3.enter_context(
                    nc.sbuf_tensor(f"rbf2{j}", [128, TTW], BF16))
                    for j in range(KC)]
                wbc_t = es3.enter_context(
                    nc.sbuf_tensor("wbc_t", [128, TTW], F32))
                onehe = onehe_a[0:8, :]
                rt8 = rt8_t[0:8, :]

                def emit_ffn(tag, rhs_hm, w1_ready, w2_ready, b1_tile,
                             combine, pre=None):
                    for tt in range(TTN):
                        pm = pre(tt) if pre else None
                        @blk.sync
                        def _(sy, tt=tt, pm=pm):
                            if pm:
                                ctx.wait(sy, pm)
                            if tt == 0:
                                ctx.wait_total(
                                    sy, ctx.group_of(rhs_hm[0, 0]))
                            for k in range(KC):
                                ctx.gw(sy, f"rbf2{k}")
                                dep = ctx.dma(sy, sLD, rbf2[k][:],
                                              rhs_hm[k, tt])
                                ctx.gsetw(f"rbf2{k}", dep)
                            ctx.mark(tag + f"_rhs_{tt}", sLD)
                        for m in range(32):
                            bank = m % 4
                            @blk.tensor
                            def _(te, tt=tt, m=m, bank=bank):
                                if tt == 0 and m == 0:
                                    ctx.wait(te, w1_ready)
                                ctx.wait(te, tag + f"_rhs_{tt}")
                                ctx.gw(te, f"ps{bank}")
                                for kk in range(KC):
                                    inst = te.matmul(
                                        ps[bank][:, :],
                                        wA[kk][:, ts(m, 128)],
                                        rbf2[kk][:],
                                        start=(kk == 0), stop=(kk == KC - 1))
                                dep = ctx.inc(inst, sMM)
                                ctx.gsetw(f"ps{bank}", dep)
                                ctx.mark(tag + f"_p1_{tt}_{m}", sMM)
                            @blk.scalar
                            def _(sc, tt=tt, m=m, bank=bank):
                                ctx.wait(sc, tag + f"_p1_{tt}_{m}")
                                ctx.gw(sc, f"evb{m}")
                                sc.activation(evb[m][:], ps[bank][:, :],
                                              AF.Gelu,
                                              bias=b1_tile[:, m:m + 1])
                                dep = ctx.inc(sc.drain(), sAC)
                                ctx.gsetw(f"evb{m}", dep)
                                ctx.gaddr(f"ps{bank}", dep)
                                ctx.mark(tag + f"_h_{tt}_{m}", sAC)
                        for d in range(KC):
                            bank = d % 4
                            @blk.tensor
                            def _(te, tt=tt, d=d, bank=bank):
                                if tt == 0 and d == 0:
                                    ctx.wait(te, w2_ready)
                                if d == 0:
                                    ctx.wait(te, tag + f"_h_{tt}_31")
                                ctx.gw(te, f"ps{bank}")
                                for m in range(32):
                                    inst = te.matmul(
                                        ps[bank][:, :],
                                        wB[m][:, ts(d, 128)], evb[m][:],
                                        start=(m == 0), stop=(m == 31))
                                dep = ctx.inc(inst, sMM)
                                ctx.gsetw(f"ps{bank}", dep)
                                for m in range(32):
                                    ctx.gaddr(f"evb{m}", dep)
                                ctx.mark(tag + f"_p2_{tt}_{d}", sMM)
                            combine(tt, d, bank)
                    ctx.marks[tag + "_done"] = (sMM, ctx.cnt["sMM"])

                @blk.gpsimd
                def _(g):
                    ctx.wait(g, f"wf_mm_{TTN - 1}_{KC - 1}")
                    if f"dn_mm_{TTN - 1}_{KC - 1}" in ctx.marks:
                        ctx.wait(g, f"dn_mm_{TTN - 1}_{KC - 1}")
                    for k in range(KC):
                        ctx.dma(g, sGD, wA[k][:, 0:4096],
                                w_ffn1T[ts(k, 128), :])
                    for m in range(32):
                        ctx.dma(g, sGD, wB[m][:, 0:1024],
                                w_ffn2T[ts(m, 128), :])
                    ctx.mark("w_l1ffn", sGD)

                def l1ffn_combine(tt, d, bank):
                    @blk.scalar
                    def _(sc, tt=tt, d=d, bank=bank):
                        ctx.wait(sc, f"F1_p2_{tt}_{d}")
                        ctx.gw(sc, EVN(8))
                        sc.activation(ev[8][:], ps[bank][:, :], AF.Identity,
                                      scale=0.1, bias=bf2s[:, d:d + 1])
                        dep = ctx.inc(sc.drain(), sAC)
                        ctx.gsetw(EVN(8), dep)
                        ctx.gaddr(f"ps{bank}", dep)
                        ctx.mark(f"s1e_{tt}_{d}", sAC)
                    @blk.sync
                    def _(sy, tt=tt, d=d):
                        if tt == 0 and d == 0:
                            ctx.wait_total(sy, "x2_h")
                        ctx.gw(sy, EVN(9))
                        dep = ctx.dma(sy, sLD, ev[9][:], x2_h[d, tt])
                        ctx.gsetw(EVN(9), dep)
                        ctx.mark(f"s1x_{tt}_{d}", sLD)
                    @blk.vector
                    def _(v, tt=tt, d=d):
                        ctx.wait(v, f"s1e_{tt}_{d}")
                        ctx.wait(v, f"s1x_{tt}_{d}")
                        v.tensor_add(ev[8][:], ev[8][:], ev[9][:])
                        dep = ctx.inc(v.drain(), sDV)
                        ctx.gsetw(EVN(8), dep)
                        ctx.gaddr(EVN(9), dep)
                        ctx.mark(f"s1_{tt}_{d}", sDV)
                    @blk.sync
                    def _(sy, tt=tt, d=d):
                        ctx.wait(sy, f"s1_{tt}_{d}")
                        dep = ctx.dma(sy, sLD, s1_h[d, tt], ev[8][:])
                        ctx.gaddr(EVN(8), dep)
                        ctx.mark(f"s1sp_{tt}_{d}", sLD)

                emit_ffn("F1", ln3bf_h, "w_l1ffn", "w_l1ffn", bf1,
                         l1ffn_combine,
                         pre=lambda tt: f"ln3sp_{tt}_{KC - 1}")

                wrF = es3.enter_context(
                    nc.sbuf_tensor("wrF", [128, 64], F32))

                @blk.sync
                def _(sy):
                    for k in range(KC):
                        ctx.dma(sy, sLD, wrF[:, ts(k, 8)], wrT[ts(k, 128), :])
                    ctx.mark("w_rt", sLD)

                for tt in range(TTN):
                    @blk.sync
                    def _(sy, tt=tt):
                        if tt == 0:
                            ctx.wait_total(sy, "xn1_h")
                        for k in range(KC):
                            ctx.gw(sy, EVN(k))
                            dep = ctx.dma(sy, sLD, ev[k][:], xn1_h[k, tt])
                            ctx.gsetw(EVN(k), dep)
                        ctx.mark(f"rt_rhs_{tt}", sLD)
                    @blk.tensor
                    def _(te, tt=tt):
                        ctx.wait(te, "w_rt")
                        ctx.wait(te, f"rt_rhs_{tt}")
                        ctx.gw(te, "psS")
                        for kk in range(KC):
                            inst = te.matmul(psS[0:8, :], wrF[:, ts(kk, 8)],
                                             ev[kk][:],
                                             start=(kk == 0),
                                             stop=(kk == KC - 1))
                        dep = ctx.inc(inst, sMM)
                        ctx.gsetw("psS", dep)
                        for kk in range(KC):
                            ctx.gaddr(EVN(kk), dep)
                        ctx.mark(f"rt_mm_{tt}", sMM)
                    @blk.vector
                    def _(v, tt=tt):
                        ctx.wait(v, f"rt_mm_{tt}")
                        v.tensor_copy(rt8[:, :], psS[0:8, :])
                        dep = ctx.inc(v.drain(), sDV)
                        ctx.gaddr("psS", dep)
                        ctx.mark(f"rt_l_{tt}", sDV)
                    @blk.tensor
                    def _(te, tt=tt):
                        ctx.wait(te, f"rt_l_{tt}")
                        ctx.gw(te, "psB")
                        for j in range(4):
                            inst = te.transpose(psB[:, j * 8:j * 8 + 8],
                                                rt8[0:8, ts(j, 128)],
                                                idf[0:8, 0:8])
                        dep = ctx.inc(inst, sMM)
                        ctx.gsetw("psB", dep)
                        ctx.mark(f"rt_tp_{tt}", sMM)
                    @blk.vector
                    def _(v, tt=tt):
                        ctx.wait(v, f"rt_tp_{tt}")
                        v.tensor_copy(rt32[:, :], psB[:, 0:32])
                        dep = ctx.inc(v.drain(), sDV)
                        ctx.gaddr("psB", dep)
                        ctx.mark(f"rt_32_{tt}", sDV)
                    for j in range(4):
                        @blk.vector
                        def _(v, tt=tt, j=j):
                            if j == 0:
                                ctx.wait(v, f"rt_32_{tt}")
                            w = rt32[:, ts(j, 8)]
                            v.tensor_reduce(mx[:, 8:9], w, AX.X, ALU.max)
                            v.memset(mx[:, 33:40], -2e30)
                            v.drain()
                            v.tensor_copy(mx[:, 32:33], mx[:, 8:9])
                            v.drain()
                            v.match_replace(mx[:, 16:24], mx[:, 32:40], w,
                                            -1e30)
                            v.drain()
                            v.tensor_reduce(mx[:, 9:10], mx[:, 16:24], AX.X,
                                            ALU.max)
                            v.drain()
                            v.tensor_copy(mx[:, 32:33], mx[:, 9:10])
                            v.drain()
                            v.match_replace(mx[:, 40:48], mx[:, 32:40],
                                            mx[:, 16:24], -1e30)
                            v.drain()
                            v.tensor_add(mx[:, 10:11], mx[:, 8:9],
                                         mx[:, 9:10])
                            v.drain()
                            v.tensor_scalar(mx[:, 10:11], mx[:, 10:11], -1.0,
                                            None, ALU.mult)
                            v.tensor_scalar(mx[:, 16:24], mx[:, 40:48],
                                            -1e29, None, ALU.is_lt)
                            v.drain()
                            ctx.inc(v.drain(), sDV)
                            ctx.mark(f"rtj_{tt}_{j}", sDV)
                        @blk.scalar
                        def _(sc, tt=tt, j=j):
                            ctx.wait(sc, f"rtj_{tt}_{j}")
                            sc.activation(mx[:, 24:32], rt32[:, ts(j, 8)],
                                          AF.Sigmoid, scale=2.0,
                                          bias=mx[:, 10:11])
                            ctx.inc(sc.drain(), sAC)
                            ctx.mark(f"rtg_{tt}_{j}", sAC)
                        @blk.vector
                        def _(v, tt=tt, j=j):
                            ctx.wait(v, f"rtg_{tt}_{j}")
                            v.tensor_mul(rt32[:, ts(j, 8)], mx[:, 24:32],
                                         mx[:, 16:24])
                            ctx.inc(v.drain(), sDV)
                            ctx.mark(f"rtw_{tt}_{j}", sDV)
                    @blk.tensor
                    def _(te, tt=tt):
                        ctx.wait(te, f"rtw_{tt}_3")
                        ctx.gw(te, "psB")
                        for j in range(4):
                            inst = te.transpose(psB[0:8, ts(j, 128)],
                                                rt32[:, ts(j, 8)], idf[:])
                        dep = ctx.inc(inst, sMM)
                        ctx.gsetw("psB", dep)
                        ctx.mark(f"rt_tpb_{tt}", sMM)
                    @blk.vector
                    def _(v, tt=tt):
                        ctx.wait(v, f"rt_tpb_{tt}")
                        v.tensor_copy(rt8[0:8, :], psB[0:8, 0:512])
                        dep = ctx.inc(v.drain(), sDV)
                        ctx.gaddr("psB", dep)
                        ctx.mark(f"rt_w_{tt}", sDV)
                    @blk.sync
                    def _(sy, tt=tt):
                        ctx.wait(sy, f"rt_w_{tt}")
                        ctx.dma(sy, sLD, wem_h[0:8, ts(tt, TTW)], rt8[0:8, :])
                        ctx.mark(f"rt_sp_{tt}", sLD)

                for e in range(E):
                    @blk.gpsimd
                    def _(g, e=e):
                        if e == 0:
                            ctx.wait(g, "F1_done")
                            ctx.wait(g, f"rt_mm_{TTN - 1}")
                        else:
                            ctx.wait(g, f"E{e - 1}_p1_{TTN - 1}_31")
                        for k in range(KC):
                            ctx.dma(g, sGD, wA[k][:, 0:4096],
                                    w1T[e, ts(k, 128), :])
                        ctx.mark(f"w1_{e}", sGD)
                        if e == 0:
                            pass
                        if e > 0:
                            ctx.wait(g, f"E{e - 1}_p2_{TTN - 1}_{KC - 1}")
                        for m in range(32):
                            ctx.dma(g, sGD, wB[m][:, 0:1024],
                                    w2T[e, ts(m, 128), :])
                        ctx.mark(f"w2_{e}", sGD)
                    @blk.sync
                    def _(sy, e=e):
                        if e == 0:
                            ctx.wait(sy, "F1_done")
                        else:
                            ctx.wait(sy, f"E{e - 1}_p2_{TTN - 1}_{KC - 1}")
                        ctx.dma(sy, sLD, b1e[:], b1_c[e])
                        ctx.dma(sy, sLD, b2e[:], b2_c[e])
                        ctx.dma(sy, sLD, onehe[:], onehot8[:, ts(e, 128)])
                        ctx.mark(f"be_{e}", sLD)

                    def moe_combine(tt, d, bank, e=e):
                        @blk.vector
                        def _(v, tt=tt, d=d, bank=bank, e=e):
                            ctx.wait(v, f"E{e}_p2_{tt}_{d}")
                            if d == 0:
                                ctx.wait(v, f"wbc_{e}_{tt}")
                                if e > 0:
                                    ctx.wait(v, f"accld_{e}_{tt}")
                            ctx.gw(v, EVN(d))
                            if e == 0:
                                v.scalar_tensor_tensor(
                                    ev[d][:], ps[bank][:, :],
                                    b2e[:, d:d + 1], wbc_t[:],
                                    ALU.add, ALU.mult)
                            else:
                                ctx.gw(v, EVN(8))
                                v.scalar_tensor_tensor(
                                    ev[8][:], ps[bank][:, :],
                                    b2e[:, d:d + 1], wbc_t[:],
                                    ALU.add, ALU.mult)
                                v.tensor_add(ev[d][:], ev[d][:], ev[8][:])
                            dep = ctx.inc(v.drain(), sDV)
                            ctx.gsetw(EVN(d), dep)
                            if e > 0:
                                ctx.gsetw(EVN(8), dep)
                            ctx.gaddr(f"ps{bank}", dep)
                            ctx.mark(f"acc_{e}_{tt}_{d}", sDV)
                        @blk.sync
                        def _(sy, tt=tt, d=d, e=e):
                            ctx.wait(sy, f"acc_{e}_{tt}_{d}")
                            dep = ctx.dma(sy, sLD, acc_h[d, tt], ev[d][:])
                            ctx.gaddr(EVN(d), dep)
                            ctx.mark(f"accsp_{e}_{tt}_{d}", sLD)

                    def moe_pre(tt, e=e):
                        @blk.sync
                        def _(sy, tt=tt, e=e):
                            ctx.wait(sy, f"rt_sp_{tt}")
                            ctx.wait(sy, f"be_{e}")
                            ctx.dma(sy, sLD, rt8[0:8, :],
                                    wem_h[0:8, ts(tt, TTW)])
                            ctx.mark(f"wemld_{e}_{tt}", sLD)
                        @blk.tensor
                        def _(te, tt=tt, e=e):
                            ctx.wait(te, f"wemld_{e}_{tt}")
                            ctx.gw(te, "psB")
                            te.matmul(psB[:, :], onehe[:], rt8[0:8, :],
                                      start=True, stop=True)
                            dep = ctx.inc(te.drain(), sMM)
                            ctx.gsetw("psB", dep)
                            ctx.mark(f"wbcm_{e}_{tt}", sMM)
                        @blk.vector
                        def _(v, tt=tt, e=e):
                            ctx.wait(v, f"wbcm_{e}_{tt}")
                            ctx.gw(v, "wbc_t")
                            v.tensor_copy(wbc_t[:], psB[:, :])
                            dep = ctx.inc(v.drain(), sDV)
                            ctx.gsetw("wbc_t", dep)
                            ctx.gaddr("psB", dep)
                            ctx.mark(f"wbc_{e}_{tt}", sDV)
                        if e > 0:
                            @blk.sync
                            def _(sy, tt=tt, e=e):
                                ctx.wait(sy, f"accsp_{e - 1}_{tt}_{KC - 1}")
                                for d in range(KC):
                                    ctx.gw(sy, EVN(d))
                                    dep = ctx.dma(sy, sLD, ev[d][:],
                                                  acc_h[d, tt])
                                    ctx.gsetw(EVN(d), dep)
                                ctx.mark(f"accld_{e}_{tt}", sLD)
                        return f"rt_sp_{tt}"

                    emit_ffn(f"E{e}", xn1bf_h, f"w1_{e}", f"w2_{e}", b1e,
                             moe_combine, pre=moe_pre)

            for tt in range(TTN):
                for k in range(KC):
                    @blk.sync
                    def _(sy, tt=tt, k=k):
                        ctx.wait(sy, f"accsp_{E - 1}_{tt}_{KC - 1}")
                        if tt == 0 and k == 0:
                            ctx.wait_total(sy, "x1_h")
                            ctx.wait_total(sy, "s1_h")
                        for evj, hsrc in ((0, acc_h), (1, x1_h), (2, s1_h)):
                            ctx.gw(sy, EVN(evj))
                            dep = ctx.dma(sy, sLD, ev[evj][:], hsrc[k, tt])
                            ctx.gsetw(EVN(evj), dep)
                        ctx.mark(f"fld_{tt}_{k}", sLD)
                    @blk.scalar
                    def _(sc, tt=tt, k=k):
                        ctx.wait(sc, f"fld_{tt}_{k}")
                        ctx.gw(sc, EVN(3))
                        sc.activation(ev[3][:], ev[0][:], AF.Silu)
                        dep = ctx.inc(sc.drain(), sAC)
                        ctx.gsetw(EVN(3), dep)
                        ctx.gaddr(EVN(0), dep)
                        ctx.mark(f"fsil_{tt}_{k}", sAC)
                    @blk.vector
                    def _(v, tt=tt, k=k):
                        ctx.wait(v, f"fsil_{tt}_{k}")
                        v.tensor_mul(ev[3][:], ev[3][:], ev[2][:])
                        v.tensor_add(ev[3][:], ev[3][:], ev[1][:])
                        dep = ctx.inc(v.drain(), sDV)
                        ctx.gsetw(EVN(3), dep)
                        ctx.gaddr(EVN(1), dep)
                        ctx.gaddr(EVN(2), dep)
                        ctx.mark(f"fout_{tt}_{k}", sDV)
                    @blk.sync
                    def _(sy, tt=tt, k=k):
                        ctx.wait(sy, f"fout_{tt}_{k}")
                        dep = ctx.dma(sy, sLD, out_fm[k][:, ts(tt, TTW)],
                                      ev[3][:])
                        ctx.gaddr(EVN(3), dep)
                        ctx.mark(f"osp_{tt}_{k}", sLD)
        else:
            @blk.vector
            def _(v):
                ctx.gw(v, EVN(0))
                v.memset(ev[0][:], 0.0)
                dep = ctx.inc(v.drain(), sDV)
                ctx.gsetw(EVN(0), dep)
                ctx.mark("zout", sDV)
            @blk.sync
            def _(sy):
                ctx.wait(sy, "zout")
                for k in range(KC):
                    for tt in range(TTN):
                        ctx.dma(sy, sLD, out_fm[k][:, ts(tt, TTW)], ev[0][:])
                ctx.mark("osp", sLD)

        final = [(s, ctx.cnt[s.name]) for s in ctx.sems.values()]
        @blk.gpsimd
        def _(g):
            for s, v in final:
                if v > 0:
                    g.wait_ge(s, v)

    return nc



_cache = {}


def _make_in_maps(inputs):
    x = np.asarray(inputs["L0_attn_x"], np.float32)
    base = {}
    base["w_qkvaT"] = np.ascontiguousarray(
        np.asarray(inputs["L0_attn_w_qkva"], np.float32).T)
    base["gamma_c"] = np.ascontiguousarray(
        np.asarray(inputs["L0_attn_gamma"], np.float32).reshape(KC, 128).T)
    base["wup_n"] = np.ascontiguousarray(
        np.asarray(inputs["L0_ffn_w_up"], np.float32))
    base["wgate_n"] = np.ascontiguousarray(
        np.asarray(inputs["L0_ffn_w_gate"], np.float32))
    base["wdown_n"] = np.ascontiguousarray(
        np.asarray(inputs["L0_ffn_w_down"], np.float32))
    base["w_toT"] = np.ascontiguousarray(
        np.asarray(inputs["L1_attn_W_to"], np.float32).T)
    base["w_fromT"] = np.ascontiguousarray(
        np.asarray(inputs["L1_attn_W_from"], np.float32).T)
    base["ldecay_c"] = np.ascontiguousarray(
        np.asarray(inputs["L1_attn_log_decay"], np.float32).reshape(2, 128).T)
    base["freq_c"] = np.ascontiguousarray(
        np.asarray(inputs["L1_attn_frequencies"], np.float32
                   ).reshape(2, 128).T)
    for nm, key in (("ln1g_c", "L1_attn_ln1_g"), ("ln1b_c", "L1_attn_ln1_b"),
                    ("ln3g_c", "L1_attn_ln3_g"), ("ln3b_c", "L1_attn_ln3_b"),
                    ("b_ffn2_c", "L1_attn_b_ffn2")):
        base[nm] = np.ascontiguousarray(
            np.asarray(inputs[key], np.float32).reshape(KC, 128).T)
    base["b_ffn1_c"] = np.ascontiguousarray(
        np.asarray(inputs["L1_attn_b_ffn1"], np.float32).reshape(32, 128).T)
    base["w_ffn1T"] = np.ascontiguousarray(
        np.asarray(inputs["L1_attn_W_ffn1"], np.float32).T)
    base["w_ffn2T"] = np.ascontiguousarray(
        np.asarray(inputs["L1_attn_W_ffn2"], np.float32).T)
    base["wrT"] = np.ascontiguousarray(
        np.asarray(inputs["L1_ffn_Wr"], np.float32).T)
    base["w1T"] = np.ascontiguousarray(
        np.transpose(np.asarray(inputs["L1_ffn_W1"], np.float32), (0, 2, 1)))
    base["w2T"] = np.ascontiguousarray(
        np.transpose(np.asarray(inputs["L1_ffn_W2"], np.float32), (0, 2, 1)))
    base["b1_c"] = np.ascontiguousarray(
        np.asarray(inputs["L1_ffn_b1"], np.float32
                   ).reshape(E, 32, 128).transpose(0, 2, 1))
    base["b2_c"] = np.ascontiguousarray(
        np.asarray(inputs["L1_ffn_b2"], np.float32
                   ).reshape(E, KC, 128).transpose(0, 2, 1))
    oh = np.zeros((8, 8, 128), np.float32)
    for e in range(8):
        oh[e, e, :] = 1.0
    base["onehot8"] = np.ascontiguousarray(
        oh.transpose(1, 0, 2).reshape(8, 1024))

    in_maps = []
    for c in range(NCORES):
        b, h = c >> 1, c & 1
        sl = x[b, h * T:(h + 1) * T, :]
        x_fm = np.ascontiguousarray(sl.T.reshape(KC, 128, T))
        cmask_a = np.zeros((128, 192), np.float32)
        my = np.zeros((128, 8, 8), np.float32)
        my[:, :, c] = 1.0
        cmask_a[:, 0:64] = my.reshape(128, 64)
        pm = np.zeros((128, 8, 8), np.float32)
        if h == 1:
            pm[:, :, c - 1] = 1.0
        cmask_a[:, 64:128] = pm.reshape(128, 64)
        mz = np.zeros((128, 4, 8), np.float32)
        mz[:, :, c] = 1.0
        cmask_a[:, 128:160] = mz.reshape(128, 32)
        pz = np.zeros((128, 4, 8), np.float32)
        if h == 1:
            pz[:, :, c - 1] = 1.0
        cmask_a[:, 160:192] = pz.reshape(128, 32)
        m = dict(base)
        m["x_fm"] = x_fm
        m["cmask"] = cmask_a
        in_maps.append(m)
    return in_maps


def kernel(**inputs):
    key = ("prog", STAGE, tuple(sorted(DEBUG)))
    if key not in _cache:
        _cache[key] = build_program(debug=DEBUG, stage=STAGE)
    nc = _cache[key]
    in_maps = _make_in_maps(inputs)
    res = run_bass_kernel_spmd(nc, in_maps, core_ids=list(range(NCORES)))
    out = np.empty((4, 4096, D), np.float32)
    for c in range(NCORES):
        b, h = c >> 1, c & 1
        o = np.asarray(res.results[c]["out_fm"]).reshape(D, T)
        out[b, h * T:(h + 1) * T, :] = o.T
    kernel.last_results = res
    return out


# version 02 (final)
# speedup vs baseline: 1.1670x; 1.1670x over previous
import re
import numpy as np
from contextlib import ExitStack

import concourse.bass as bass
import concourse.mybir as mybir
from concourse.bass_utils import run_bass_kernel_spmd

F32 = mybir.dt.float32
BF16 = mybir.dt.bfloat16
I32 = mybir.dt.int32
AF = mybir.ActivationFunctionType
ALU = mybir.AluOpType
AX = mybir.AxisListType
ts = bass.ts

NCORES = 8
D = 1024
T = 2048
KC = 8
TTN = 4
TTW = 512
E = 8
QMAX_A = 32767.0
PI_2 = 1.5707963267948966

DEBUG = ()
STAGE = 5


class Ctx:

    PREFIX = re.compile(r"^(wA|wB|rbf2|rbf|wfF|evb|hb|xtt)\d+$")

    def __init__(self, nc, es):
        self.nc = nc
        self.es = es
        self.cnt = {}
        self.marks = {}
        self.gd = {}
        self.sems = {}
        self.pending = {"sync": [], "gpsimd": [], "vector": []}
        self.total_groups = set()

    def sem(self, es, name):
        s = es.enter_context(self.nc.semaphore(name))
        self.cnt[name] = 0
        self.sems[name] = s
        return s

    def gsem(self, group):
        name = "D_" + group
        if name not in self.sems:
            s = self.es.enter_context(self.nc.semaphore(name))
            self.sems[name] = s
            self.cnt[name] = 0
        return self.sems[name]

    def group_of(self, out_ap):
        n = out_ap.tensor.name
        m = self.PREFIX.match(n)
        return m.group(1) if m else n

    def dma(self, eng, sem_sentinel, out, in_):
        g = self.group_of(out)
        s = self.gsem(g)
        eng.dma_start(out=out, in_=in_).then_inc(s, 16)
        self.cnt[s.name] += 16
        if sem_sentinel is self._gd_sentinel:
            q = "gpsimd"
        elif sem_sentinel is self._vd_sentinel:
            q = "vector"
        else:
            q = "sync"
        self.pending[q].append((s, self.cnt[s.name]))
        return (s, self.cnt[s.name])

    def inc(self, inst, sem, n=1):
        inst.then_inc(sem, n)
        self.cnt[sem.name] += n
        return (sem, self.cnt[sem.name])

    def mark(self, name, sem):
        if sem is self._ld_sentinel or sem is self._gd_sentinel \
                or sem is self._vd_sentinel:
            q = ("gpsimd" if sem is self._gd_sentinel else
                 "vector" if sem is self._vd_sentinel else "sync")
            merged = {}
            for s, v in self.pending[q]:
                merged[s.name] = (s, max(v, merged.get(s.name, (s, 0))[1]))
            self.marks[name] = list(merged.values())
            self.pending[q] = []
        else:
            self.marks[name] = [(sem, self.cnt[sem.name])]

    def _resolve(self, dep):
        sem, v = dep
        if sem.name.startswith("D_") and                 sem.name[2:] in self.total_groups:
            v = self.cnt[sem.name]
        return sem, v

    def wait(self, eng, name):
        deps = self.marks[name]
        if isinstance(deps, tuple):
            deps = [deps]
        for dep in deps:
            sem, v = self._resolve(dep)
            if v > 0:
                eng.wait_ge(sem, v)

    def wdep(self, eng, dep):
        if dep is None:
            return
        if isinstance(dep, list):
            for d in dep:
                self.wdep(eng, d)
            return
        sem, v = self._resolve(dep)
        if v > 0:
            eng.wait_ge(sem, v)

    def wait_total(self, eng, group):
        s = self.gsem(group)
        if self.cnt[s.name] > 0:
            eng.wait_ge(s, self.cnt[s.name])

    def gw(self, eng, name):
        for d in self.gd.get(name, []):
            self.wdep(eng, d)

    def gsetw(self, name, dep):
        self.gd[name] = [dep]

    def gr(self, eng, name):
        lst = self.gd.get(name)
        if lst:
            self.wdep(eng, lst[0])

    def gaddr(self, name, dep):
        self.gd.setdefault(name, []).append(dep)


def build_program(debug=(), stage=STAGE):
    nc = bass.Bass("TRN2", target_bir_lowering=False, debug=True,
                   detect_race_conditions=False)

    EIN = lambda n, s, d=F32: nc.dram_tensor(n, s, d, kind="ExternalInput").ap()
    xin = EIN("x_fm", [KC, 128, T])
    cmask = EIN("cmask", [128, 192])
    onehot8 = EIN("onehot8", [8, 1024])
    w_qkvaT = EIN("w_qkvaT", [D, 3072])
    gamma_c = EIN("gamma_c", [128, KC])
    wup_n = EIN("wup_n", [3072, D])
    wgate_n = EIN("wgate_n", [3072, D])
    wdown_n = EIN("wdown_n", [D, 3072])
    w_toT = EIN("w_toT", [D, 512])
    w_fromT = EIN("w_fromT", [512, D])
    ldecay_c = EIN("ldecay_c", [128, 2])
    freq_c = EIN("freq_c", [128, 2])
    ln1g_c = EIN("ln1g_c", [128, KC])
    ln1b_c = EIN("ln1b_c", [128, KC])
    ln3g_c = EIN("ln3g_c", [128, KC])
    ln3b_c = EIN("ln3b_c", [128, KC])
    w_ffn1T = EIN("w_ffn1T", [D, 4096])
    b_ffn1_c = EIN("b_ffn1_c", [128, 32])
    w_ffn2T = EIN("w_ffn2T", [4096, D])
    b_ffn2_c = EIN("b_ffn2_c", [128, KC])
    wrT = EIN("wrT", [D, 8])
    w1T = EIN("w1T", [E, D, 4096])
    b1_c = EIN("b1_c", [E, 128, 32])
    w2T = EIN("w2T", [E, 4096, D])
    b2_c = EIN("b2_c", [E, 128, KC])

    out_fm = nc.dram_tensor("out_fm", [KC, 128, T], F32,
                            kind="ExternalOutput").ap()

    def dram(name, shape, dtype=F32):
        kind = "ExternalOutput" if name in debug else "Internal"
        return nc.dram_tensor(name, shape, dtype, kind=kind).ap()

    def blocked(name, nk=KC, dtype=F32):
        return dram(name, [nk, TTN, 128, TTW], dtype)

    xn_h = blocked("xn_h")
    q_h = blocked("q_h")
    kv_h = blocked("kv_h")
    a_h = blocked("a_h")
    s_h = blocked("s_h")
    up_h = blocked("up_h", 24)
    gate_h = blocked("gate_h", 24)
    g2_h = blocked("g2_h", 24)
    obf_h = blocked("obf_h", 24, BF16)
    x1_h = blocked("x1_h")
    xn1_h = blocked("xn1_h")
    xn1bf_h = blocked("xn1bf_h", KC, BF16)
    ln1bf_h = blocked("ln1bf_h", KC, BF16)
    x2_h = blocked("x2_h")
    ln3bf_h = blocked("ln3bf_h", KC, BF16)
    s1_h = blocked("s1_h")
    acc_h = blocked("acc_h")
    spo_h = dram("spo_h", [128, 4 * T]) if "spo_h" in debug else None
    wem_h = dram("wem_h", [8, T])
    lpdbg_h = dram("lpdbg_h", [24, 128, 1024], BF16) if "lpdbg_h" in debug \
        else None
    scdbg_h = dram("scdbg_h", [128, 32]) if "scdbg_h" in debug else None

    ccm_i = nc.dram_tensor("ccm_i", [1, 16], F32)
    ccm_o = nc.dram_tensor("ccm_o", [1, 16], F32)
    ccy_i = nc.dram_tensor("ccy_i", [128, 64], F32)
    ccy_o = nc.dram_tensor("ccy_o", [128, 64], F32)
    ccz_i = nc.dram_tensor("ccz_i", [128, 32], F32)
    ccz_o = nc.dram_tensor("ccz_o", [128, 32], F32)

    with ExitStack() as es0:
        ent = lambda t: es0.enter_context(t)
        ctx = Ctx(nc, es0)

        smN = [0]
        smT = ent(nc.sbuf_tensor("smT", [128, 896], F32))

        def sm(n):
            a = smT[:, smN[0]:smN[0] + n]
            smN[0] += n
            return a

        cm = sm(192)
        gam = sm(KC)
        l1g = sm(KC)
        l1b = sm(KC)
        l3g = sm(KC)
        l3b = sm(KC)
        bf2s = sm(KC)
        bf1 = sm(32)
        ldec = sm(2)
        freq = sm(2)
        b1e = sm(32)
        b2e = sm(KC)
        scol = sm(16)
        mx = sm(48)
        ystate = sm(KC)
        ybnd = sm(KC)
        _ycc_unused = sm(0)
        zstA = sm(4)
        zbnd = sm(4)
        _zcc_unused = sm(0)
        cfCD = sm(4)
        cfSD = sm(4)
        cfA = sm(4)
        cfB = sm(4)
        cfc = sm(2)
        cfs = sm(2)
        cfd = sm(2)
        wsc = sm(96)
        wrs = sm(96)
        wng = sm(96)
        rt32 = sm(32)
        cst = sm(2)
        onehe_a = sm(128)
        assert smN[0] <= 896, smN[0]
        smRT = ent(nc.sbuf_tensor("smRT", [1, 3 * TTW], F32))
        srow = smRT[0:1, 0:TTW]
        srow2 = smRT[0:1, TTW:2 * TTW]
        srow3 = smRT[0:1, 2 * TTW:3 * TTW]
        scal_t = ent(nc.sbuf_tensor("scal_t", [1, 32], F32))
        scal = scal_t[0:1, :]
        ycc = ent(nc.sbuf_tensor("ycc_t", [128, 64], F32))
        zcc = ent(nc.sbuf_tensor("zcc_t", [128, 32], F32))
        rt8_t = ent(nc.sbuf_tensor("rt8_t", [8, TTW], F32))
        idb = ent(nc.sbuf_tensor("idb", [128, 128], BF16))
        idf = ent(nc.sbuf_tensor("idf", [128, 128], F32))
        ones_k = ent(nc.sbuf_tensor("ones_k", [128, 1], F32))
        ones_b = ent(nc.sbuf_tensor("ones_b", [1, 128], F32))


        NEV = 10
        ev = [ent(nc.sbuf_tensor(f"ev{j}", [128, TTW], F32)) for j in range(NEV)]
        qi_t = ent(nc.sbuf_tensor("qi_t", [128, TTW], I32))
        wdq0 = ent(nc.sbuf_tensor("wdq0", [128, TTW], BF16))
        wdq1 = ent(nc.sbuf_tensor("wdq1", [128, TTW], BF16))

        wA = [ent(nc.sbuf_tensor(f"wA{k}", [128, 4096], BF16)) for k in range(KC)]

        ps = [ent(nc.psum_tensor(f"ps{j}", [128, TTW], F32)) for j in range(4)]
        psS = ent(nc.psum_tensor("psS", [128, TTW], F32))
        psQ = ent(nc.psum_tensor("psQ", [128, TTW], F32))
        psB = ent(nc.psum_tensor("psB", [128, TTW], F32))
        psTb = ent(nc.psum_tensor("psTb", [128, 1024], BF16))

        sMM = ctx.sem(es0, "sMM")
        sDV = ctx.sem(es0, "sDV")
        sAC = ctx.sem(es0, "sAC")
        sGP = ctx.sem(es0, "sGP")
        sCC = ctx.sem(es0, "sCC")
        sLD = object()
        sGD = object()
        sVD = object()
        ctx._ld_sentinel = sLD
        ctx._gd_sentinel = sGD
        ctx._vd_sentinel = sVD
        ctx.total_groups = {
            "xn_h", "q_h", "kv_h", "a_h", "s_h", "up_h", "gate_h", "g2_h",
            "obf_h", "x1_h", "xn1_h", "xn1bf_h", "ln1bf_h", "x2_h",
            "ln3bf_h", "s1_h", "acc_h", "wem_h", "out_fm", "spo_h"}

        blk = ent(nc.Block())

        def EVN(j):
            return f"ev{j}"

        def emit_quant(v, src_ap, out_ap, rs_ap, negs_ap, m_t, f_t):
            v.tensor_scalar(m_t[:], src_ap, rs_ap, None, ALU.mult)
            v.tensor_copy(qi_t[:], m_t[:])
            v.tensor_copy(f_t[:], qi_t[:])
            v.tensor_sub(m_t[:], m_t[:], f_t[:])
            v.scalar_tensor_tensor(f_t[:], m_t[:], 0.5, f_t[:],
                                   ALU.is_gt, ALU.add)
            v.scalar_tensor_tensor(m_t[:], m_t[:], -0.5, f_t[:],
                                   ALU.is_lt, ALU.subtract)
            v.tensor_scalar(out_ap, m_t[:], negs_ap, None, ALU.mult)

        def emit_collapse_max(j0, slot, tag, pre):
            @blk.tensor
            def _(te):
                ctx.wait(te, pre)
                ctx.gw(te, "psB")
                te.transpose(psB[0:1, 0:128], mx[:, j0:j0 + 1], idf[:])
                dep = ctx.inc(te.drain(), sMM)
                ctx.gsetw("psB", dep)
                ctx.mark(tag + "_c2", sMM)
            @blk.vector
            def _(v):
                ctx.wait(v, tag + "_c2")
                v.tensor_reduce(scal[0:1, slot:slot + 1], psB[0:1, 0:128],
                                AX.X, ALU.max)
                dep = ctx.inc(v.drain(), sDV)
                ctx.gaddr("psB", dep)
                ctx.mark(tag + "_c3", sDV)

        def emit_allreduce_scal(tag, pre):
            @blk.sync
            def _(sy):
                ctx.wait(sy, pre)
                ctx.dma(sy, sLD, ccm_i.ap(), scal[0:1, 0:16])
                ctx.mark(tag + "_in", sLD)
            @blk.gpsimd
            def _(g):
                ctx.wait(g, tag + "_in")
                ctx.inc(g.collective_compute(
                    "AllReduce", ALU.max,
                    replica_groups=[list(range(NCORES))],
                    ins=[ccm_i.ap().opt()], outs=[ccm_o.ap().opt()]), sCC)
                g.wait_ge(sCC, ctx.cnt["sCC"])
                ctx.inc(g.drain(), sGP)
                ctx.mark(tag + "_cc", sGP)
            @blk.sync
            def _(sy):
                ctx.wait(sy, tag + "_cc")
                ctx.dma(sy, sLD, scal[0:1, 0:16], ccm_o.ap())
                ctx.mark(tag + "_out", sLD)

        def emit_allreduce_add(tag, cc_i, cc_o, src, dst, pre):
            @blk.sync
            def _(sy):
                ctx.wait(sy, pre)
                ctx.dma(sy, sLD, cc_i.ap(), src)
                ctx.mark(tag + "_in", sLD)
            @blk.gpsimd
            def _(g):
                ctx.wait(g, tag + "_in")
                ctx.inc(g.collective_compute(
                    "AllReduce", ALU.add,
                    replica_groups=[list(range(NCORES))],
                    ins=[cc_i.ap().opt()], outs=[cc_o.ap().opt()]), sCC)
                g.wait_ge(sCC, ctx.cnt["sCC"])
                ctx.inc(g.drain(), sGP)
                ctx.mark(tag + "_cc", sGP)
            @blk.sync
            def _(sy):
                ctx.wait(sy, tag + "_cc")
                ctx.dma(sy, sLD, dst, cc_o.ap())
                ctx.mark(tag + "_out", sLD)

        def emit_scale_cols(slot, c0, tag, pre):
            @blk.vector
            def _(v):
                ctx.wait(v, pre)
                v.tensor_scalar(scal[0:1, slot:slot + 1],
                                scal[0:1, slot:slot + 1], 1.0 / QMAX_A, 1e-12,
                                ALU.mult, ALU.max)
                v.drain()
                v.reciprocal(scal[0:1, 24:25], scal[0:1, slot:slot + 1])
                v.drain()
                v.tensor_scalar(scal[0:1, 25:26], scal[0:1, slot:slot + 1],
                                -1.0, None, ALU.mult)
                ctx.inc(v.drain(), sDV)
                ctx.mark(tag + "_s", sDV)
            @blk.tensor
            def _(te):
                ctx.wait(te, tag + "_s")
                ctx.gw(te, "psB")
                te.matmul(psB[:, 0:2], ones_b[:], scal[0:1, 24:26],
                          start=True, stop=True)
                dep = ctx.inc(te.drain(), sMM)
                ctx.gsetw("psB", dep)
                ctx.mark(tag + "_b", sMM)
            @blk.vector
            def _(v):
                ctx.wait(v, tag + "_b")
                v.tensor_copy(scol[:, c0:c0 + 2], psB[:, 0:2])
                dep = ctx.inc(v.drain(), sDV)
                ctx.gaddr("psB", dep)
                ctx.mark(tag + "_cols", sDV)

        @blk.sync
        def _(sy):
            ctx.dma(sy, sLD, cm[:], cmask)
            ctx.dma(sy, sLD, gam[:], gamma_c)
            ctx.dma(sy, sLD, l1g[:], ln1g_c)
            ctx.dma(sy, sLD, l1b[:], ln1b_c)
            ctx.dma(sy, sLD, l3g[:], ln3g_c)
            ctx.dma(sy, sLD, l3b[:], ln3b_c)
            ctx.dma(sy, sLD, bf1[:], b_ffn1_c)
            ctx.dma(sy, sLD, bf2s[:], b_ffn2_c)
            ctx.dma(sy, sLD, ldec[:], ldecay_c)
            ctx.dma(sy, sLD, freq[:], freq_c)
            ctx.mark("p0_small", sLD)

        @blk.vector
        def _(v):
            v.memset(idb[:], 0.0)
            v.memset(idf[:], 0.0)
            ctx.inc(v.drain(), sDV)
            ctx.mark("p0_idz", sDV)

        @blk.gpsimd
        def _(g):
            ctx.wait(g, "p0_idz")
            g.affine_select(out=idb[:], in_=idb[:], compare_op=ALU.not_equal,
                            fill=1.0, base=0, pattern=[[-1, 128]],
                            channel_multiplier=1)
            g.affine_select(out=idf[:], in_=idf[:], compare_op=ALU.not_equal,
                            fill=1.0, base=0, pattern=[[-1, 128]],
                            channel_multiplier=1)
            ctx.inc(g.drain(), sGP)
            ctx.mark("p0_ident", sGP)
            for k in range(KC):
                ctx.dma(g, sGD, wA[k][:, 0:3072], w_qkvaT[ts(k, 128), :])
            ctx.mark("w_qkva", sGD)

        nc.const_aps.register(F32, 1e-5, cst[:, 0:1])
        nc.const_aps.register(F32, PI_2, cst[:, 1:2])

        @blk.vector
        def _(v):
            ctx.wait(v, "p0_small")
            v.memset(cst[:, 0:1], 1e-5)
            v.memset(cst[:, 1:2], PI_2)
            v.memset(ones_k[:], 1.0)
            v.memset(ones_b[:], 1.0)
            v.memset(ystate[:], 0.0)
            v.memset(zstA[:], 0.0)
            v.memset(mx[:], 0.0)
            v.memset(scal[:], 0.0)
            v.tensor_scalar(bf2s[:], bf2s[:], 0.1, None, ALU.mult)
            ctx.inc(v.drain(), sDV)
            ctx.mark("p0_ones", sDV)

        def emit_ln_tt(tag, src_dram_fn, pre_mark, want_cs, apply_k,
                       eps=1e-5):
            for k in range(KC):
                lj = k % 2
                sq = ev[8] if lj == 0 else ev[9]
                @blk.sync
                def _(sy, k=k, lj=lj):
                    if pre_mark:
                        ctx.wait(sy, pre_mark)
                    ctx.gw(sy, EVN(lj))
                    dep = ctx.dma(sy, sLD, ev[lj][:], src_dram_fn(k))
                    ctx.gsetw(EVN(lj), dep)
                    ctx.mark(tag + f"_ld_{k}", sLD)
                @blk.scalar
                def _(sc, k=k, lj=lj, sq=sq):
                    ctx.wait(sc, tag + f"_ld_{k}")
                    ctx.gw(sc, EVN(8 + lj))
                    sc.activation(sq[:], ev[lj][:], AF.Square)
                    dep = ctx.inc(sc.drain(), sAC)
                    ctx.gsetw(EVN(8 + lj), dep)
                    ctx.gaddr(EVN(lj), dep)
                    ctx.mark(tag + f"_sq_{k}", sAC)
                @blk.tensor
                def _(te, k=k, lj=lj, sq=sq):
                    ctx.wait(te, tag + f"_sq_{k}")
                    if k == 0:
                        ctx.gw(te, "psS")
                        ctx.gw(te, "psQ")
                    te.matmul(psS[0:1, :], ones_k[:], ev[lj][:],
                              start=(k == 0), stop=(k == KC - 1))
                    inst = te.matmul(psQ[0:1, :], ones_k[:], sq[:],
                                     start=(k == 0), stop=(k == KC - 1))
                    dep = ctx.inc(inst, sMM)
                    ctx.gaddr(EVN(lj), dep)
                    ctx.gaddr(EVN(8 + lj), dep)
                    if k == KC - 1:
                        ctx.gsetw("psS", dep)
                        ctx.gsetw("psQ", dep)
                        ctx.mark(tag + "_stats", sMM)

            @blk.vector
            def _(v):
                ctx.wait(v, tag + "_stats")
                v.tensor_scalar(srow[0:1, :], psS[0:1, :], 1.0 / D, None,
                                ALU.mult)
                v.tensor_scalar(srow2[0:1, :], psQ[0:1, :], 1.0 / D, None,
                                ALU.mult)
                v.tensor_mul(srow3[0:1, :], srow[0:1, :], srow[0:1, :])
                v.tensor_sub(srow2[0:1, :], srow2[0:1, :], srow3[0:1, :])
                v.tensor_scalar(srow2[0:1, :], srow2[0:1, :], 1e-37, None,
                                ALU.max)
                dep = ctx.inc(v.drain(), sDV)
                ctx.gaddr("psS", dep)
                ctx.gaddr("psQ", dep)
                ctx.mark(tag + "_var", sDV)

            @blk.scalar
            def _(sc):
                ctx.wait(sc, tag + "_var")
                if want_cs:
                    sc.activation(srow3[0:1, :], srow2[0:1, :], AF.Ln)
                    sc.activation(srow3[0:1, :], srow3[0:1, :], AF.Exp,
                                  scale=-0.5)
                sc.activation(srow2[0:1, :], srow2[0:1, :], AF.Ln, bias=eps)
                sc.activation(srow2[0:1, :], srow2[0:1, :], AF.Exp, scale=-0.5)
                ctx.inc(sc.drain(), sAC)
                ctx.mark(tag + "_istd", sAC)

            rows = [(srow, 5), (srow2, 6)] + ([(srow3, 7)] if want_cs else [])
            for ri, (rrow, evj) in enumerate(rows):
                @blk.tensor
                def _(te, ri=ri, rrow=rrow, evj=evj):
                    ctx.wait(te, tag + "_istd")
                    ctx.gw(te, "psB")
                    te.matmul(psB[:, :], ones_b[:], rrow[0:1, :],
                              start=True, stop=True)
                    dep = ctx.inc(te.drain(), sMM)
                    ctx.gsetw("psB", dep)
                    ctx.mark(tag + f"_bcm{ri}", sMM)
                @blk.vector
                def _(v, ri=ri, evj=evj):
                    ctx.wait(v, tag + f"_bcm{ri}")
                    ctx.gw(v, EVN(evj))
                    v.tensor_copy(ev[evj][:], psB[:, :])
                    dep = ctx.inc(v.drain(), sDV)
                    ctx.gsetw(EVN(evj), dep)
                    ctx.gaddr("psB", dep)
                    ctx.mark(tag + f"_bc{ri}", sDV)

            for k in range(KC):
                lj = 2 + (k % 2)
                @blk.sync
                def _(sy, k=k, lj=lj):
                    ctx.gw(sy, EVN(lj))
                    dep = ctx.dma(sy, sLD, ev[lj][:], src_dram_fn(k))
                    ctx.gsetw(EVN(lj), dep)
                    ctx.mark(tag + f"_ld2_{k}", sLD)
                @blk.vector
                def _(v, k=k, lj=lj):
                    ctx.wait(v, tag + f"_ld2_{k}")
                    if k == 0:
                        ctx.wait(v, tag + f"_bc{len(rows) - 1}")
                    ctx.gw(v, EVN(4))
                    v.tensor_sub(ev[4][:], ev[lj][:], ev[5][:])
                    dep = ctx.inc(v.drain(), sDV)
                    ctx.gsetw(EVN(4), dep)
                    ctx.gaddr(EVN(lj), dep)
                    ctx.mark(tag + f"_xc_{k}", sDV)
                apply_k(k)
            ctx.marks[tag + "_app"] = (sDV, ctx.cnt["sDV"])

        with ExitStack() as es1:
            wB = [es1.enter_context(nc.sbuf_tensor(f"wB{k}", [128, 1024],
                                                   BF16)) for k in range(32)]
            hb = [es1.enter_context(nc.sbuf_tensor(f"hb{k}", [128, T], BF16))
                  for k in range(KC)]

            for tt in range(TTN):
                def l0_apply(k, tt=tt):
                    @blk.vector
                    def _(v, k=k, tt=tt):
                        ctx.wait(v, f"L0LN{tt}_xc_{k}")
                        dj = k % 2
                        ctx.gw(v, EVN(dj))
                        v.tensor_mul(ev[dj][:], ev[4][:], ev[6][:])
                        v.scalar_tensor_tensor(hb[k][:, ts(tt, TTW)], ev[4][:],
                                               gam[:, k:k + 1], ev[7][:],
                                               ALU.mult, ALU.mult)
                        v.tensor_reduce(mx[:, 0:1], ev[dj][:], AX.X, ALU.max,
                                        apply_absolute_value=True)
                        v.drain()
                        if tt == 0 and k == 0:
                            v.tensor_copy(mx[:, 1:2], mx[:, 0:1])
                        else:
                            v.tensor_max(mx[:, 1:2], mx[:, 1:2], mx[:, 0:1])
                        dep = ctx.inc(v.drain(), sDV)
                        ctx.gsetw(EVN(dj), dep)
                        ctx.gaddr(EVN(4), dep)
                        ctx.mark(f"l0ap_{tt}_{k}", sDV)
                    @blk.sync
                    def _(sy, k=k, tt=tt):
                        ctx.wait(sy, f"l0ap_{tt}_{k}")
                        dj = k % 2
                        dep = ctx.dma(sy, sLD, xn_h[k, tt], ev[dj][:])
                        ctx.gaddr(EVN(dj), dep)

                emit_ln_tt(f"L0LN{tt}", lambda k, tt=tt: xin[k][:, ts(tt, TTW)],
                           "p0_ones" if tt == 0 else None, True, l0_apply)

            @blk.vector
            def _(v):
                ctx.wait(v, f"l0ap_{TTN - 1}_{KC - 1}")
                ctx.inc(v.drain(), sDV)
                ctx.mark("xnmax_done", sDV)
            emit_collapse_max(1, 0, "axn", "xnmax_done")
            emit_allreduce_scal("axn_ar", "axn_c3")
            emit_scale_cols(0, 0, "sxn", "axn_ar_out")

            if stage >= 2:
                for tt in range(TTN):
                    for k in range(KC):
                        for role, mc in (("kv", 8 + k), ("a", 16 + k)):
                            bank = (tt * 16 + k * 2 + (0 if role == "kv" else 1)) % 4
                            @blk.tensor
                            def _(te, tt=tt, mc=mc, bank=bank):
                                ctx.wait(te, "w_qkva")
                                ctx.wait(te, f"L0LN{tt}_app")
                                ctx.gw(te, f"ps{bank}")
                                for kk in range(KC):
                                    inst = te.matmul(
                                        ps[bank][:, :],
                                        wA[kk][:, ts(mc, 128)],
                                        hb[kk][:, ts(tt, TTW)],
                                        start=(kk == 0), stop=(kk == KC - 1))
                                dep = ctx.inc(inst, sMM)
                                ctx.gsetw(f"ps{bank}", dep)
                                ctx.mark(f"qk_mm_{tt}_{mc}", sMM)
                            evj = 2 if role == "kv" else 3
                            @blk.scalar
                            def _(sc, tt=tt, mc=mc, bank=bank, evj=evj,
                                  role=role):
                                ctx.wait(sc, f"qk_mm_{tt}_{mc}")
                                ctx.gw(sc, EVN(evj))
                                if role == "kv":
                                    sc.copy(ev[evj][:], ps[bank][:, :])
                                else:
                                    sc.activation(ev[evj][:], ps[bank][:, :],
                                                  AF.Sigmoid)
                                dep = ctx.inc(sc.drain(), sAC)
                                ctx.gsetw(EVN(evj), dep)
                                ctx.gaddr(f"ps{bank}", dep)
                                ctx.mark(f"qk_ev_{tt}_{mc}", sAC)
                            @blk.sync
                            def _(sy, tt=tt, k=k, mc=mc, evj=evj, role=role):
                                ctx.wait(sy, f"qk_ev_{tt}_{mc}")
                                dsth = kv_h if role == "kv" else a_h
                                dep = ctx.dma(sy, sLD, dsth[k, tt], ev[evj][:])
                                ctx.gaddr(EVN(evj), dep)
                        @blk.vector
                        def _(v, tt=tt, k=k):
                            ctx.wait(v, f"qk_ev_{tt}_{16 + k}")
                            ctx.gw(v, EVN(5))
                            v.tensor_tensor_scan(ev[5][:], ev[3][:], ev[2][:],
                                                 ystate[:, k:k + 1],
                                                 ALU.mult, ALU.add)
                            v.drain()
                            v.tensor_copy(ystate[:, k:k + 1],
                                          ev[5][:, TTW - 1:TTW])
                            dep = ctx.inc(v.drain(), sDV)
                            ctx.gsetw(EVN(5), dep)
                            ctx.gaddr(EVN(2), dep)
                            ctx.gaddr(EVN(3), dep)
                            ctx.mark(f"scan1_{tt}_{k}", sDV)
                    for k in range(KC):
                        mc = k
                        bank = k % 4
                        @blk.tensor
                        def _(te, tt=tt, mc=mc, bank=bank):
                            ctx.gw(te, f"ps{bank}")
                            for kk in range(KC):
                                inst = te.matmul(
                                    ps[bank][:, :], wA[kk][:, ts(mc, 128)],
                                    hb[kk][:, ts(tt, TTW)],
                                    start=(kk == 0), stop=(kk == KC - 1))
                            dep = ctx.inc(inst, sMM)
                            ctx.gsetw(f"ps{bank}", dep)
                            ctx.mark(f"q_mm_{tt}_{mc}", sMM)
                        evj = 2 + (k % 2)
                        @blk.scalar
                        def _(sc, tt=tt, mc=mc, bank=bank, evj=evj):
                            ctx.wait(sc, f"q_mm_{tt}_{mc}")
                            ctx.gw(sc, EVN(evj))
                            sc.copy(ev[evj][:], ps[bank][:, :])
                            dep = ctx.inc(sc.drain(), sAC)
                            ctx.gsetw(EVN(evj), dep)
                            ctx.gaddr(f"ps{bank}", dep)
                            ctx.mark(f"q_ev_{tt}_{mc}", sAC)
                        @blk.sync
                        def _(sy, tt=tt, mc=mc, evj=evj):
                            ctx.wait(sy, f"q_ev_{tt}_{mc}")
                            dep = ctx.dma(sy, sLD, q_h[mc, tt], ev[evj][:])
                            ctx.gaddr(EVN(evj), dep)

                @blk.tensor
                def _(te):
                    ctx.inc(te.drain(), sMM)
                    ctx.mark("qkva_mm_done", sMM)

                @blk.vector
                def _(v):
                    ctx.wait(v, f"scan1_{TTN - 1}_{KC - 1}")
                    for c in range(KC):
                        v.tensor_scalar(ycc[:, ts(c, 8)], cm[:, ts(c, 8)],
                                        ystate[:, c:c + 1], None, ALU.mult)
                    ctx.inc(v.drain(), sDV)
                    ctx.mark("ycc_ready", sDV)
                emit_allreduce_add("yar", ccy_i, ccy_o, ycc[:], ycc[:],
                                   "ycc_ready")
                @blk.vector
                def _(v):
                    ctx.wait(v, "yar_out")
                    v.tensor_mul(ycc[:], ycc[:], cm[:, 64:128])
                    v.tensor_reduce(ybnd[:],
                                    ycc[:].rearrange("p (c s) -> p c s", s=8),
                                    AX.X, ALU.add)
                    ctx.inc(v.drain(), sDV)
                    ctx.mark("ybnd_ready", sDV)

            if stage >= 3:
                for tt in range(TTN):
                    for k in range(KC):
                        @blk.sync
                        def _(sy, tt=tt, k=k):
                            ctx.wait(sy, "sxn_cols")
                            if tt == 0 and k == 0:
                                ctx.wait_total(sy, "xn_h")
                            ctx.gw(sy, EVN(6))
                            dep = ctx.dma(sy, sLD, ev[6][:], xn_h[k, tt])
                            ctx.gsetw(EVN(6), dep)
                            ctx.mark(f"xqld_{tt}_{k}", sLD)
                        @blk.vector
                        def _(v, tt=tt, k=k):
                            ctx.wait(v, f"xqld_{tt}_{k}")
                            if tt == 0 and k == 0:
                                ctx.wait(v, "qkva_mm_done")
                            ctx.gw(v, EVN(8))
                            ctx.gw(v, EVN(9))
                            emit_quant(v, ev[6][:], hb[k][:, ts(tt, TTW)],
                                       scol[:, 0:1], scol[:, 1:2],
                                       ev[8], ev[9])
                            dep = ctx.inc(v.drain(), sDV)
                            for j in (6, 8, 9):
                                ctx.gsetw(EVN(j), dep)
                            ctx.mark(f"xq_{tt}_{k}", sDV)
                ctx.mark("xq_done", sDV)

                def emit_lpbq(tag, w_nat, O, I, dst_put, wfree_mark):
                    nblk = I // 32
                    for ot in range(O // 128):
                        for pc in range(I // TTW):
                            @blk.sync
                            def _(sy, ot=ot, pc=pc):
                                if wfree_mark and ot == 0 and pc == 0:
                                    pass
                                ctx.gw(sy, EVN(2))
                                dep = ctx.dma(sy, sLD, ev[2][:],
                                              w_nat[ts(ot, 128), ts(pc, TTW)])
                                ctx.gsetw(EVN(2), dep)
                                ctx.mark(tag + f"_ldn_{ot}_{pc}", sLD)
                            @blk.vector
                            def _(v, ot=ot, pc=pc):
                                ctx.wait(v, tag + f"_ldn_{ot}_{pc}")
                                nb = TTW // 32
                                v.tensor_reduce(
                                    wsc[:, pc * nb:(pc + 1) * nb],
                                    ev[2][:].rearrange("p (b i) -> p b i",
                                                       i=32),
                                    AX.X, ALU.max, apply_absolute_value=True)
                                dep = ctx.inc(v.drain(), sDV)
                                ctx.gaddr(EVN(2), dep)
                                ctx.mark(tag + f"_sc_{ot}_{pc}", sDV)
                        @blk.vector
                        def _(v, ot=ot):
                            ctx.wait(v, tag + f"_sc_{ot}_{I // TTW - 1}")
                            v.tensor_scalar(wsc[:, 0:nblk], wsc[:, 0:nblk],
                                            1.0 / 7.0, 1e-12, ALU.mult,
                                            ALU.max)
                            v.drain()
                            v.reciprocal(wrs[:, 0:nblk], wsc[:, 0:nblk])
                            v.tensor_scalar(wng[:, 0:nblk], wsc[:, 0:nblk],
                                            -1.0, None, ALU.mult)
                            ctx.inc(v.drain(), sDV)
                            ctx.mark(tag + f"_rs_{ot}", sDV)
                        for pc in range(I // TTW):
                            wdn = "wdq0" if (pc % 2 == 0) else "wdq1"
                            wd = wdq0 if (pc % 2 == 0) else wdq1
                            nb = TTW // 32
                            @blk.sync
                            def _(sy, ot=ot, pc=pc):
                                ctx.gw(sy, EVN(2))
                                dep = ctx.dma(sy, sLD, ev[2][:],
                                              w_nat[ts(ot, 128), ts(pc, TTW)])
                                ctx.gsetw(EVN(2), dep)
                                ctx.mark(tag + f"_ldn2_{ot}_{pc}", sLD)
                            @blk.vector
                            def _(v, ot=ot, pc=pc, wd=wd, wdn=wdn, nb=nb):
                                ctx.wait(v, tag + f"_ldn2_{ot}_{pc}")
                                ctx.wait(v, tag + f"_rs_{ot}")
                                ctx.gw(v, wdn)
                                ctx.gw(v, EVN(3))
                                ctx.gw(v, EVN(4))
                                rsv = wrs[:, pc * nb:(pc + 1) * nb].rearrange(
                                    "p (b o) -> p b o", o=1).broadcast_to(
                                    [128, nb, 32])
                                ngv = wng[:, pc * nb:(pc + 1) * nb].rearrange(
                                    "p (b o) -> p b o", o=1).broadcast_to(
                                    [128, nb, 32])
                                e2 = ev[2][:].rearrange("p (b i) -> p b i",
                                                        i=32)
                                e3 = ev[3][:].rearrange("p (b i) -> p b i",
                                                        i=32)
                                v.tensor_tensor(e3, e2, rsv, ALU.mult)
                                v.tensor_copy(qi_t[:], ev[3][:])
                                v.tensor_copy(ev[4][:], qi_t[:])
                                v.tensor_sub(ev[3][:], ev[3][:], ev[4][:])
                                v.scalar_tensor_tensor(ev[4][:], ev[3][:],
                                                       0.5, ev[4][:],
                                                       ALU.is_gt, ALU.add)
                                v.scalar_tensor_tensor(ev[3][:], ev[3][:],
                                                       -0.5, ev[4][:],
                                                       ALU.is_lt,
                                                       ALU.subtract)
                                e3b = ev[3][:].rearrange("p (b i) -> p b i",
                                                         i=32)
                                wdv = wd[:].rearrange("p (b i) -> p b i",
                                                      i=32)
                                v.tensor_tensor(wdv, e3b, ngv, ALU.mult)
                                dep = ctx.inc(v.drain(), sDV)
                                for nme in (EVN(2), EVN(3), EVN(4), wdn):
                                    ctx.gsetw(nme, dep)
                                ctx.mark(tag + f"_dq_{ot}_{pc}", sDV)
                            if lpdbg_h is not None and tag == "lpup":
                                @blk.sync
                                def _(sy, ot=ot, pc=pc, wd=wd):
                                    ctx.wait(sy, tag + f"_dq_{ot}_{pc}")
                                    ctx.dma(sy, sLD,
                                            lpdbg_h[ot, :, ts(pc, TTW)],
                                            wd[:])
                            @blk.tensor
                            def _(te, ot=ot, pc=pc, wd=wd, wdn=wdn):
                                ctx.wait(te, tag + f"_dq_{ot}_{pc}")
                                if wfree_mark:
                                    ctx.wait(te, wfree_mark)
                                ctx.gw(te, "psTb")
                                for jj in range(4):
                                    inst = te.transpose(
                                        psTb[:, ts(jj, 128)],
                                        wd[:, ts(jj, 128)], idb[:])
                                dep = ctx.inc(inst, sMM)
                                ctx.gsetw("psTb", dep)
                                ctx.gaddr(wdn, dep)
                                ctx.mark(tag + f"_tp_{ot}_{pc}", sMM)
                            @blk.scalar
                            def _(sc, ot=ot, pc=pc):
                                ctx.wait(sc, tag + f"_tp_{ot}_{pc}")
                                for jj in range(4):
                                    ic = pc * 4 + jj
                                    sc.copy(dst_put(ic, ot),
                                            psTb[:, ts(jj, 128)])
                                dep = ctx.inc(sc.drain(), sAC)
                                ctx.gaddr("psTb", dep)
                                ctx.mark(tag + f"_tpe_{ot}_{pc}", sAC)
                    ctx.mark(tag + "_done", sAC)

                def up_put(ic, ot):
                    return wA[ic][:, ts(ot, 128)]

                def gate_put(ic, ot):
                    return wB[ic * 3 + ot // 8][:, (ot % 8) * 128:
                                                (ot % 8) * 128 + 128]

                emit_lpbq("lpup", wup_n, 3072, D, up_put, "qkva_mm_done")
                emit_lpbq("lpgt", wgate_n, 3072, D, gate_put, None)

                for wi, (wtag, outh, mslot, wget) in enumerate((
                        ("up", up_h, 1,
                         lambda kk, mc: wA[kk][:, ts(mc, 128)]),
                        ("gt", gate_h, 2,
                         lambda kk, mc: wB[kk * 3 + mc // 8][:,
                                        (mc % 8) * 128:(mc % 8) * 128 + 128]))):
                    wmark = "lpup_done" if wtag == "up" else "lpgt_done"
                    mxj = 2 + wi
                    first = True
                    for tt in range(TTN):
                        for mc in range(24):
                            bank = (tt * 24 + mc) % 4
                            @blk.tensor
                            def _(te, tt=tt, mc=mc, bank=bank, wget=wget,
                                  wmark=wmark):
                                ctx.wait(te, wmark)
                                if tt == 0 and mc == 0:
                                    ctx.wait(te, "xq_done")
                                ctx.gw(te, f"ps{bank}")
                                for kk in range(KC):
                                    inst = te.matmul(
                                        ps[bank][:, :], wget(kk, mc),
                                        hb[kk][:, ts(tt, TTW)],
                                        start=(kk == 0), stop=(kk == KC - 1))
                                dep = ctx.inc(inst, sMM)
                                ctx.gsetw(f"ps{bank}", dep)
                                ctx.mark(f"{wtag}_mm_{tt}_{mc}", sMM)
                            evj = 2 + (mc % 2)
                            @blk.scalar
                            def _(sc, tt=tt, mc=mc, bank=bank, evj=evj,
                                  wtag=wtag):
                                ctx.wait(sc, f"{wtag}_mm_{tt}_{mc}")
                                ctx.gw(sc, EVN(evj))
                                sc.copy(ev[evj][:], ps[bank][:, :])
                                dep = ctx.inc(sc.drain(), sAC)
                                ctx.gsetw(EVN(evj), dep)
                                ctx.gaddr(f"ps{bank}", dep)
                                ctx.mark(f"{wtag}_ev_{tt}_{mc}", sAC)
                            @blk.vector
                            def _(v, tt=tt, mc=mc, evj=evj, wtag=wtag,
                                  mxj=mxj, first=first):
                                ctx.wait(v, f"{wtag}_ev_{tt}_{mc}")
                                v.tensor_reduce(mx[:, 0:1], ev[evj][:], AX.X,
                                                ALU.max,
                                                apply_absolute_value=True)
                                v.drain()
                                if first:
                                    v.tensor_copy(mx[:, mxj:mxj + 1],
                                                  mx[:, 0:1])
                                else:
                                    v.tensor_max(mx[:, mxj:mxj + 1],
                                                 mx[:, mxj:mxj + 1],
                                                 mx[:, 0:1])
                                dep = ctx.inc(v.drain(), sDV)
                                ctx.gaddr(EVN(evj), dep)
                                ctx.mark(f"{wtag}_mx_{tt}_{mc}", sDV)
                            first = False
                            @blk.sync
                            def _(sy, tt=tt, mc=mc, evj=evj, outh=outh,
                                  wtag=wtag):
                                ctx.wait(sy, f"{wtag}_mx_{tt}_{mc}")
                                dep = ctx.dma(sy, sLD, outh[mc, tt],
                                              ev[evj][:])
                                ctx.gaddr(EVN(evj), dep)
                                ctx.mark(f"{wtag}_sp_{tt}_{mc}", sLD)

                emit_collapse_max(2, 1, "aup", f"up_mx_{TTN - 1}_23")
                emit_collapse_max(3, 2, "agt", f"gt_mx_{TTN - 1}_23")
                emit_allreduce_scal("ug_ar", "agt_c3")
                emit_scale_cols(1, 2, "sup", "ug_ar_out")
                emit_scale_cols(2, 4, "sgt", "sup_cols")

                if scdbg_h is not None:
                    @blk.sync
                    def _(sy):
                        ctx.wait(sy, "sgt_cols")
                        ctx.dma(sy, sLD, scdbg_h[:, 0:16], scol[:, 0:16])
                        ctx.dma(sy, sLD, scdbg_h[0:1, 16:32], scal[0:1, 0:16])

                first = True
                for mc in range(24):
                    for tt in range(TTN):
                        @blk.sync
                        def _(sy, mc=mc, tt=tt):
                            if mc == 0 and tt == 0:
                                ctx.wait(sy, "sgt_cols")
                                ctx.wait_total(sy, "gate_h")
                            ctx.gw(sy, EVN(2))
                            dep = ctx.dma(sy, sLD, ev[2][:], gate_h[mc, tt])
                            ctx.gsetw(EVN(2), dep)
                            ctx.mark(f"g2ld_{mc}_{tt}", sLD)
                        @blk.vector
                        def _(v, mc=mc, tt=tt):
                            ctx.wait(v, f"g2ld_{mc}_{tt}")
                            for j in (3, 8, 9):
                                ctx.gw(v, EVN(j))
                            emit_quant(v, ev[2][:], ev[3][:], scol[:, 4:5],
                                       scol[:, 5:6], ev[8], ev[9])
                            dep = ctx.inc(v.drain(), sDV)
                            for j in (2, 3, 8, 9):
                                ctx.gsetw(EVN(j), dep) if j != 2 else \
                                    ctx.gaddr(EVN(2), dep)
                            ctx.mark(f"gdq_{mc}_{tt}", sDV)
                        @blk.scalar
                        def _(sc, mc=mc, tt=tt):
                            ctx.wait(sc, f"gdq_{mc}_{tt}")
                            ctx.gw(sc, EVN(6))
                            sc.activation(ev[6][:], ev[3][:], AF.Sigmoid)
                            dep = ctx.inc(sc.drain(), sAC)
                            ctx.gsetw(EVN(6), dep)
                            ctx.gaddr(EVN(3), dep)
                            ctx.mark(f"sig_{mc}_{tt}", sAC)
                        @blk.vector
                        def _(v, mc=mc, tt=tt, first=first):
                            ctx.wait(v, f"sig_{mc}_{tt}")
                            for j in (8, 9):
                                ctx.gw(v, EVN(j))
                            v.tensor_scalar(ev[8][:], ev[6][:], 65536.0, None,
                                            ALU.mult)
                            v.tensor_copy(qi_t[:], ev[8][:])
                            v.tensor_copy(ev[9][:], qi_t[:])
                            v.tensor_sub(ev[8][:], ev[8][:], ev[9][:])
                            v.scalar_tensor_tensor(ev[9][:], ev[8][:], 0.5,
                                                   ev[9][:], ALU.is_gt,
                                                   ALU.add)
                            v.scalar_tensor_tensor(ev[9][:], ev[8][:], -0.5,
                                                   ev[9][:], ALU.is_lt,
                                                   ALU.subtract)
                            v.tensor_scalar(ev[9][:], ev[9][:], -65535.0,
                                            -1.0 / 65536.0, ALU.max, ALU.mult)
                            v.tensor_mul(ev[6][:], ev[3][:], ev[9][:])
                            v.tensor_reduce(mx[:, 0:1], ev[6][:], AX.X,
                                            ALU.max,
                                            apply_absolute_value=True)
                            v.drain()
                            if first:
                                v.tensor_copy(mx[:, 4:5], mx[:, 0:1])
                            else:
                                v.tensor_max(mx[:, 4:5], mx[:, 4:5],
                                             mx[:, 0:1])
                            dep = ctx.inc(v.drain(), sDV)
                            for j in (6, 8, 9):
                                ctx.gsetw(EVN(j), dep)
                            ctx.gaddr(EVN(3), dep)
                            ctx.mark(f"g2_{mc}_{tt}", sDV)
                        first = False
                        @blk.sync
                        def _(sy, mc=mc, tt=tt):
                            ctx.wait(sy, f"g2_{mc}_{tt}")
                            dep = ctx.dma(sy, sLD, g2_h[mc, tt], ev[6][:])
                            ctx.gaddr(EVN(6), dep)
                            ctx.mark(f"g2sp_{mc}_{tt}", sLD)
                emit_collapse_max(4, 3, "ag2", f"g2_{23}_{TTN - 1}")
                emit_allreduce_scal("g2_ar", "ag2_c3")
                emit_scale_cols(3, 6, "sg2", "g2_ar_out")

                first = True
                for mc in range(24):
                    for tt in range(TTN):
                        @blk.sync
                        def _(sy, mc=mc, tt=tt):
                            if mc == 0 and tt == 0:
                                ctx.wait(sy, "sg2_cols")
                                ctx.wait_total(sy, "g2_h")
                                ctx.wait_total(sy, "up_h")
                            ctx.gw(sy, EVN(2))
                            dep = ctx.dma(sy, sLD, ev[2][:], g2_h[mc, tt])
                            ctx.gsetw(EVN(2), dep)
                            ctx.gw(sy, EVN(5))
                            dep = ctx.dma(sy, sLD, ev[5][:], up_h[mc, tt])
                            ctx.gsetw(EVN(5), dep)
                            ctx.mark(f"old_{mc}_{tt}", sLD)
                        @blk.vector
                        def _(v, mc=mc, tt=tt, first=first):
                            ctx.wait(v, f"old_{mc}_{tt}")
                            for j in (3, 6, 8, 9):
                                ctx.gw(v, EVN(j))
                            emit_quant(v, ev[2][:], ev[3][:], scol[:, 6:7],
                                       scol[:, 7:8], ev[8], ev[9])
                            emit_quant(v, ev[5][:], ev[6][:], scol[:, 2:3],
                                       scol[:, 3:4], ev[8], ev[9])
                            v.tensor_mul(ev[3][:], ev[3][:], ev[6][:])
                            v.tensor_reduce(mx[:, 0:1], ev[3][:], AX.X,
                                            ALU.max,
                                            apply_absolute_value=True)
                            v.drain()
                            if first:
                                v.tensor_copy(mx[:, 5:6], mx[:, 0:1])
                            else:
                                v.tensor_max(mx[:, 5:6], mx[:, 5:6],
                                             mx[:, 0:1])
                            dep = ctx.inc(v.drain(), sDV)
                            for j in (3, 6, 8, 9):
                                ctx.gsetw(EVN(j), dep)
                            ctx.gaddr(EVN(2), dep)
                            ctx.gaddr(EVN(5), dep)
                            ctx.mark(f"oraw_{mc}_{tt}", sDV)
                        first = False
                        @blk.sync
                        def _(sy, mc=mc, tt=tt):
                            ctx.wait(sy, f"oraw_{mc}_{tt}")
                            dep = ctx.dma(sy, sLD, g2_h[mc, tt], ev[3][:])
                            ctx.gaddr(EVN(3), dep)
                            ctx.mark(f"orsp_{mc}_{tt}", sLD)
                emit_collapse_max(5, 4, "ao", f"oraw_{23}_{TTN - 1}")
                emit_allreduce_scal("o_ar", "ao_c3")
                emit_scale_cols(4, 8, "so", "o_ar_out")

                for mc in range(24):
                    for tt in range(TTN):
                        @blk.sync
                        def _(sy, mc=mc, tt=tt):
                            if mc == 0 and tt == 0:
                                ctx.wait(sy, "so_cols")
                                ctx.wait_total(sy, "g2_h")
                            ctx.gw(sy, EVN(2))
                            dep = ctx.dma(sy, sLD, ev[2][:], g2_h[mc, tt])
                            ctx.gsetw(EVN(2), dep)
                            ctx.mark(f"oqld_{mc}_{tt}", sLD)
                        @blk.vector
                        def _(v, mc=mc, tt=tt):
                            ctx.wait(v, f"oqld_{mc}_{tt}")
                            for j in (8, 9):
                                ctx.gw(v, EVN(j))
                            ctx.gw(v, "wdq0")
                            emit_quant(v, ev[2][:], wdq0[:], scol[:, 8:9],
                                       scol[:, 9:10], ev[8], ev[9])
                            dep = ctx.inc(v.drain(), sDV)
                            for j in (8, 9):
                                ctx.gsetw(EVN(j), dep)
                            ctx.gaddr(EVN(2), dep)
                            ctx.gsetw("wdq0", dep)
                            ctx.mark(f"oq_{mc}_{tt}", sDV)
                        @blk.sync
                        def _(sy, mc=mc, tt=tt):
                            ctx.wait(sy, f"oq_{mc}_{tt}")
                            dep = ctx.dma(sy, sLD, obf_h[mc, tt], wdq0[:])
                            ctx.gaddr("wdq0", dep)
                            ctx.mark(f"oqsp_{mc}_{tt}", sLD)

                def down_put(ic, ot):
                    return wB[ic][:, ts(ot, 128)]

                emit_lpbq("lpdn", wdown_n, D, 3072, down_put,
                          f"gt_mm_{TTN - 1}_23")

                for tt in range(TTN):
                    for k in range(KC):
                        @blk.sync
                        def _(sy, tt=tt, k=k):
                            if tt == 0 and k == 0:
                                ctx.wait(sy, "ybnd_ready")
                                ctx.wait_total(sy, "a_h")
                                ctx.wait_total(sy, "kv_h")
                                ctx.wait_total(sy, "q_h")
                            for evj, hsrc in ((3, a_h), (5, kv_h), (6, q_h)):
                                ctx.gw(sy, EVN(evj))
                                dep = ctx.dma(sy, sLD, ev[evj][:],
                                              hsrc[k, tt])
                                ctx.gsetw(EVN(evj), dep)
                            ctx.mark(f"s2ld_{tt}_{k}", sLD)
                        @blk.vector
                        def _(v, tt=tt, k=k):
                            ctx.wait(v, f"s2ld_{tt}_{k}")
                            ctx.gw(v, EVN(7))
                            init = ybnd[:, k:k + 1] if tt == 0 else \
                                ystate[:, k:k + 1]
                            v.tensor_tensor_scan(ev[7][:], ev[3][:],
                                                 ev[5][:], init,
                                                 ALU.mult, ALU.add)
                            v.drain()
                            v.tensor_copy(ystate[:, k:k + 1],
                                          ev[7][:, TTW - 1:TTW])
                            v.tensor_mul(ev[7][:], ev[7][:], ev[6][:])
                            dep = ctx.inc(v.drain(), sDV)
                            ctx.gsetw(EVN(7), dep)
                            for j in (3, 5, 6):
                                ctx.gaddr(EVN(j), dep)
                            ctx.mark(f"smix_{tt}_{k}", sDV)
                        @blk.sync
                        def _(sy, tt=tt, k=k):
                            ctx.wait(sy, f"smix_{tt}_{k}")
                            dep = ctx.dma(sy, sLD, s_h[k, tt], ev[7][:])
                            ctx.gaddr(EVN(7), dep)
                            ctx.mark(f"ssp_{tt}_{k}", sLD)

                for tt in range(TTN):
                    for d in range(KC):
                        bank = (tt * KC + d) % 4
                        for kk in range(24):
                            wdn = "wdq0" if kk % 2 == 0 else "wdq1"
                            wd = wdq0 if kk % 2 == 0 else wdq1
                            @blk.sync
                            def _(sy, tt=tt, d=d, kk=kk, wdn=wdn, wd=wd):
                                if tt == 0 and d == 0 and kk == 0:
                                    ctx.wait_total(sy, "obf_h")
                                ctx.gw(sy, wdn)
                                dep = ctx.dma(sy, sLD, wd[:], obf_h[kk, tt])
                                ctx.gsetw(wdn, dep)
                                ctx.mark(f"dnld_{tt}_{d}_{kk}", sLD)
                            @blk.tensor
                            def _(te, tt=tt, d=d, kk=kk, bank=bank, wdn=wdn,
                                  wd=wd):
                                ctx.wait(te, f"dnld_{tt}_{d}_{kk}")
                                if kk == 0:
                                    if tt == 0 and d == 0:
                                        ctx.wait(te, "lpdn_done")
                                    ctx.gw(te, f"ps{bank}")
                                inst = te.matmul(ps[bank][:, :],
                                                 wB[kk][:, ts(d, 128)],
                                                 wd[:], start=(kk == 0),
                                                 stop=(kk == 23))
                                dep = ctx.inc(inst, sMM)
                                ctx.gaddr(wdn, dep)
                                if kk == 23:
                                    ctx.gsetw(f"ps{bank}", dep)
                                    ctx.mark(f"dn_mm_{tt}_{d}", sMM)
                        @blk.scalar
                        def _(sc, tt=tt, d=d, bank=bank):
                            ctx.wait(sc, f"dn_mm_{tt}_{d}")
                            ctx.gw(sc, EVN(4))
                            sc.activation(ev[4][:], ps[bank][:, :], AF.Silu)
                            dep = ctx.inc(sc.drain(), sAC)
                            ctx.gsetw(EVN(4), dep)
                            ctx.gaddr(f"ps{bank}", dep)
                            ctx.mark(f"dn_ev_{tt}_{d}", sAC)
                        @blk.sync
                        def _(sy, tt=tt, d=d):
                            ctx.wait(sy, f"ssp_{tt}_{d}")
                            ctx.gw(sy, EVN(6))
                            dep = ctx.dma(sy, sLD, ev[6][:], s_h[d, tt])
                            ctx.gsetw(EVN(6), dep)
                            ctx.gw(sy, EVN(7))
                            dep = ctx.dma(sy, sLD, ev[7][:],
                                          xin[d][:, ts(tt, TTW)])
                            ctx.gsetw(EVN(7), dep)
                            ctx.mark(f"x1ld_{tt}_{d}", sLD)
                        @blk.vector
                        def _(v, tt=tt, d=d):
                            ctx.wait(v, f"dn_ev_{tt}_{d}")
                            ctx.wait(v, f"x1ld_{tt}_{d}")
                            v.tensor_mul(ev[4][:], ev[4][:], ev[6][:])
                            v.tensor_add(ev[4][:], ev[4][:], ev[7][:])
                            dep = ctx.inc(v.drain(), sDV)
                            ctx.gsetw(EVN(4), dep)
                            for j in (6, 7):
                                ctx.gaddr(EVN(j), dep)
                            ctx.mark(f"x1_{tt}_{d}", sDV)
                        @blk.sync
                        def _(sy, tt=tt, d=d):
                            ctx.wait(sy, f"x1_{tt}_{d}")
                            dep = ctx.dma(sy, sLD, x1_h[d, tt], ev[4][:])
                            ctx.gaddr(EVN(4), dep)
                            ctx.mark(f"x1sp_{tt}_{d}", sLD)


        if stage >= 4:
            for tt in range(TTN):
                def l1_apply(k, tt=tt):
                    @blk.vector
                    def _(v, k=k, tt=tt):
                        ctx.wait(v, f"L1LN{tt}_xc_{k}")
                        dj = k % 2
                        ctx.gw(v, EVN(dj))
                        ctx.gw(v, "wdq1")
                        v.tensor_mul(ev[dj][:], ev[4][:], ev[6][:])
                        v.tensor_copy(wdq1[:], ev[dj][:])
                        dep = ctx.inc(v.drain(), sDV)
                        ctx.gsetw(EVN(dj), dep)
                        ctx.gsetw("wdq1", dep)
                        ctx.gaddr(EVN(4), dep)
                        ctx.mark(f"l1ap_{tt}_{k}", sDV)
                    @blk.sync
                    def _(sy, k=k, tt=tt):
                        ctx.wait(sy, f"l1ap_{tt}_{k}")
                        dj = k % 2
                        dep = ctx.dma(sy, sLD, xn1_h[k, tt], ev[dj][:])
                        ctx.gaddr(EVN(dj), dep)
                        dep = ctx.dma(sy, sLD, xn1bf_h[k, tt], wdq1[:])
                        ctx.gaddr("wdq1", dep)
                        ctx.mark(f"xn1sp_{tt}_{k}", sLD)

                emit_ln_tt(f"L1LN{tt}",
                           lambda k, tt=tt: x1_h[k, tt],
                           f"x1sp_{tt}_{KC - 1}" if stage >= 3 else None,
                           False, l1_apply)

            for tt in range(TTN):
                def ln1_apply(k, tt=tt):
                    @blk.vector
                    def _(v, k=k, tt=tt):
                        ctx.wait(v, f"LN1{tt}_xc_{k}")
                        dj = k % 2
                        ctx.gw(v, EVN(dj))
                        ctx.gw(v, "wdq1")
                        v.tensor_mul(ev[dj][:], ev[4][:], ev[6][:])
                        v.tensor_scalar(wdq1[:], ev[dj][:], l1g[:, k:k + 1],
                                        l1b[:, k:k + 1], ALU.mult, ALU.add)
                        dep = ctx.inc(v.drain(), sDV)
                        ctx.gsetw(EVN(dj), dep)
                        ctx.gsetw("wdq1", dep)
                        ctx.gaddr(EVN(4), dep)
                        ctx.mark(f"ln1ap_{tt}_{k}", sDV)
                    @blk.sync
                    def _(sy, k=k, tt=tt):
                        ctx.wait(sy, f"ln1ap_{tt}_{k}")
                        dep = ctx.dma(sy, sLD, ln1bf_h[k, tt], wdq1[:])
                        ctx.gaddr("wdq1", dep)
                        ctx.mark(f"ln1sp_{tt}_{k}", sLD)

                emit_ln_tt(f"LN1{tt}",
                           lambda k, tt=tt: xn1_h[k, tt],
                           f"xn1sp_{tt}_{KC - 1}", False, ln1_apply)

        if stage >= 5:
            with ExitStack() as es2:
                spec = es2.enter_context(
                    nc.sbuf_tensor("spec", [128, 4 * T], F32))
                rbf2 = [es2.enter_context(
                    nc.sbuf_tensor(f"rbf2{j}", [128, TTW], BF16))
                    for j in range(KC)]
                wfF = [es2.enter_context(
                    nc.sbuf_tensor(f"wfF{j}", [128, 128], F32))
                    for j in range(4)]
                evb = [es2.enter_context(
                    nc.sbuf_tensor(f"evb{j}", [128, TTW], BF16))
                    for j in range(32)]
                wbc_t = es2.enter_context(
                    nc.sbuf_tensor("wbc_t", [128, TTW], F32))
                wrF = es2.enter_context(
                    nc.sbuf_tensor("wrF", [128, 64], F32))
                NW2 = 4
                w2s = [es2.enter_context(
                    nc.sbuf_tensor(f"w2s{j}", [128, TTW], BF16))
                    for j in range(NW2)]
                ctp = [es2.enter_context(
                    nc.sbuf_tensor(f"ctp{j}", [128, TTW], F32))
                    for j in range(4)]
                onehe = onehe_a[0:8, :]
                rt8 = rt8_t[0:8, :]

                @blk.gpsimd
                def _(g):
                    ctx.wait(g, f"up_mm_{TTN - 1}_23")
                    for k in range(KC):
                        ctx.dma(g, sGD, wA[k][:, 0:512],
                                w_toT[ts(k, 128), :])
                    ctx.mark("w_to_from", sGD)

                for tt in range(TTN):
                    @blk.sync
                    def _(sy, tt=tt):
                        ctx.wait(sy, f"ln1sp_{tt}_{KC - 1}")
                        for k in range(KC):
                            ctx.gw(sy, f"rbf2{k}")
                            dep = ctx.dma(sy, sLD, rbf2[k][:],
                                          ln1bf_h[k, tt])
                            ctx.gsetw(f"rbf2{k}", dep)
                        ctx.mark(f"wt_rhs_{tt}", sLD)
                    for mc in range(4):
                        bank = (tt * 4 + mc) % 4
                        @blk.tensor
                        def _(te, tt=tt, mc=mc, bank=bank):
                            ctx.wait(te, "w_to_from")
                            ctx.wait(te, f"wt_rhs_{tt}")
                            ctx.gw(te, f"ps{bank}")
                            for kk in range(KC):
                                inst = te.matmul(
                                    ps[bank][:, :],
                                    wA[kk][:, mc * 128:mc * 128 + 128],
                                    rbf2[kk][:],
                                    start=(kk == 0), stop=(kk == KC - 1))
                            dep = ctx.inc(inst, sMM)
                            ctx.gsetw(f"ps{bank}", dep)
                            ctx.mark(f"wt_mm_{tt}_{mc}", sMM)
                        @blk.scalar
                        def _(sc, tt=tt, mc=mc, bank=bank):
                            ctx.wait(sc, f"wt_mm_{tt}_{mc}")
                            sc.activation(
                                spec[:, mc * T + tt * TTW:
                                     mc * T + tt * TTW + TTW],
                                ps[bank][:, :], AF.Identity, scale=0.1)
                            dep = ctx.inc(sc.drain(), sAC)
                            ctx.gaddr(f"ps{bank}", dep)
                            ctx.mark(f"wt_ev_{tt}_{mc}", sAC)

                @blk.scalar
                def _(sc):
                    ctx.wait(sc, "p0_ones")
                    sc.activation(cfd[:], ldec[:], AF.Sigmoid)
                    sc.activation(cfs[:], freq[:], AF.Tanh)
                    ctx.inc(sc.drain(), sAC)
                    ctx.mark("cf1", sAC)
                @blk.vector
                def _(v):
                    ctx.wait(v, "cf1")
                    v.tensor_scalar(cfs[:], cfs[:], 0.1, None, ALU.mult)
                    ctx.inc(v.drain(), sDV)
                    ctx.mark("cf2", sDV)
                @blk.scalar
                def _(sc):
                    ctx.wait(sc, "cf2")
                    sc.activation(cfc[:], cfs[:], AF.Sin, bias=PI_2)
                    sc.activation(cfs[:], cfs[:], AF.Sin)
                    ctx.inc(sc.drain(), sAC)
                    ctx.mark("cf3", sAC)
                @blk.vector
                def _(v):
                    ctx.wait(v, "cf3")
                    v.tensor_mul(cfCD[:, 0:2], cfc[:], cfd[:])
                    v.drain()
                    v.tensor_copy(cfCD[:, 2:4], cfCD[:, 0:2])
                    v.tensor_mul(cfSD[:, 2:4], cfs[:], cfd[:])
                    v.drain()
                    v.tensor_scalar(cfSD[:, 0:2], cfSD[:, 2:4], -1.0, None,
                                    ALU.mult)
                    ctx.inc(v.drain(), sDV)
                    ctx.mark("cf_done", sDV)

                @blk.sync
                def _(sy):
                    for k in range(KC):
                        ctx.dma(sy, sLD, wrF[:, ts(k, 8)], wrT[ts(k, 128), :])
                    ctx.mark("w_rt", sLD)

                for tt in range(TTN):
                    @blk.sync
                    def _(sy, tt=tt):
                        if tt == 0:
                            ctx.wait_total(sy, "xn1_h")
                        for k in range(KC):
                            ctx.gw(sy, EVN(k))
                            dep = ctx.dma(sy, sLD, ev[k][:], xn1_h[k, tt])
                            ctx.gsetw(EVN(k), dep)
                        ctx.mark(f"rt_rhs_{tt}", sLD)
                    @blk.tensor
                    def _(te, tt=tt):
                        ctx.wait(te, "w_rt")
                        ctx.wait(te, f"rt_rhs_{tt}")
                        ctx.gw(te, "psS")
                        for kk in range(KC):
                            inst = te.matmul(psS[0:8, :], wrF[:, ts(kk, 8)],
                                             ev[kk][:],
                                             start=(kk == 0),
                                             stop=(kk == KC - 1))
                        dep = ctx.inc(inst, sMM)
                        ctx.gsetw("psS", dep)
                        for kk in range(KC):
                            ctx.gaddr(EVN(kk), dep)
                        ctx.mark(f"rt_mm_{tt}", sMM)
                    @blk.vector
                    def _(v, tt=tt):
                        ctx.wait(v, f"rt_mm_{tt}")
                        v.tensor_copy(rt8[:, :], psS[0:8, :])
                        dep = ctx.inc(v.drain(), sDV)
                        ctx.gaddr("psS", dep)
                        ctx.mark(f"rt_l_{tt}", sDV)
                    @blk.tensor
                    def _(te, tt=tt):
                        ctx.wait(te, f"rt_l_{tt}")
                        ctx.gw(te, "psB")
                        for j in range(4):
                            inst = te.transpose(psB[:, j * 8:j * 8 + 8],
                                                rt8[0:8, ts(j, 128)],
                                                idf[0:8, 0:8])
                        dep = ctx.inc(inst, sMM)
                        ctx.gsetw("psB", dep)
                        ctx.mark(f"rt_tp_{tt}", sMM)
                    @blk.vector
                    def _(v, tt=tt):
                        ctx.wait(v, f"rt_tp_{tt}")
                        v.tensor_copy(rt32[:, :], psB[:, 0:32])
                        dep = ctx.inc(v.drain(), sDV)
                        ctx.gaddr("psB", dep)
                        ctx.mark(f"rt_32_{tt}", sDV)
                    for j in range(4):
                        @blk.vector
                        def _(v, tt=tt, j=j):
                            if j == 0:
                                ctx.wait(v, f"rt_32_{tt}")
                            w = rt32[:, ts(j, 8)]
                            v.tensor_reduce(mx[:, 8:9], w, AX.X, ALU.max)
                            v.memset(mx[:, 33:40], -2e30)
                            v.drain()
                            v.tensor_copy(mx[:, 32:33], mx[:, 8:9])
                            v.drain()
                            v.match_replace(mx[:, 16:24], mx[:, 32:40], w,
                                            -1e30)
                            v.drain()
                            v.tensor_reduce(mx[:, 9:10], mx[:, 16:24], AX.X,
                                            ALU.max)
                            v.drain()
                            v.tensor_copy(mx[:, 32:33], mx[:, 9:10])
                            v.drain()
                            v.match_replace(mx[:, 40:48], mx[:, 32:40],
                                            mx[:, 16:24], -1e30)
                            v.drain()
                            v.tensor_add(mx[:, 10:11], mx[:, 8:9],
                                         mx[:, 9:10])
                            v.drain()
                            v.tensor_scalar(mx[:, 10:11], mx[:, 10:11], -1.0,
                                            None, ALU.mult)
                            v.tensor_scalar(mx[:, 16:24], mx[:, 40:48],
                                            -1e29, None, ALU.is_lt)
                            v.drain()
                            ctx.inc(v.drain(), sDV)
                            ctx.mark(f"rtj_{tt}_{j}", sDV)
                        @blk.scalar
                        def _(sc, tt=tt, j=j):
                            ctx.wait(sc, f"rtj_{tt}_{j}")
                            sc.activation(mx[:, 24:32], rt32[:, ts(j, 8)],
                                          AF.Sigmoid, scale=2.0,
                                          bias=mx[:, 10:11])
                            ctx.inc(sc.drain(), sAC)
                            ctx.mark(f"rtg_{tt}_{j}", sAC)
                        @blk.vector
                        def _(v, tt=tt, j=j):
                            ctx.wait(v, f"rtg_{tt}_{j}")
                            v.tensor_mul(rt32[:, ts(j, 8)], mx[:, 24:32],
                                         mx[:, 16:24])
                            ctx.inc(v.drain(), sDV)
                            ctx.mark(f"rtw_{tt}_{j}", sDV)
                    @blk.tensor
                    def _(te, tt=tt):
                        ctx.wait(te, f"rtw_{tt}_3")
                        ctx.gw(te, "psB")
                        for j in range(4):
                            inst = te.transpose(psB[0:8, ts(j, 128)],
                                                rt32[:, ts(j, 8)], idf[:])
                        dep = ctx.inc(inst, sMM)
                        ctx.gsetw("psB", dep)
                        ctx.mark(f"rt_tpb_{tt}", sMM)
                    @blk.vector
                    def _(v, tt=tt):
                        ctx.wait(v, f"rt_tpb_{tt}")
                        v.tensor_copy(rt8[0:8, :], psB[0:8, 0:512])
                        dep = ctx.inc(v.drain(), sDV)
                        ctx.gaddr("psB", dep)
                        ctx.mark(f"rt_w_{tt}", sDV)
                    @blk.sync
                    def _(sy, tt=tt):
                        ctx.wait(sy, f"rt_w_{tt}")
                        ctx.dma(sy, sLD, wem_h[0:8, ts(tt, TTW)], rt8[0:8, :])
                        ctx.mark(f"rt_sp_{tt}", sLD)

                for half in range(2):
                    for tt in range(TTN):
                        @blk.vector
                        def _(v, half=half, tt=tt):
                            if half == 0 and tt == 0:
                                ctx.wait(v, "cf_done")
                            ctx.wait(v, f"wt_ev_{tt}_{half}")
                            ctx.wait(v, f"wt_ev_{tt}_{2 + half}")
                            sl_r = half * T + tt * TTW
                            sl_i = (2 + half) * T + tt * TTW
                            for j in (0, 1, 2, 3):
                                ctx.gw(v, EVN(j))
                            v.tensor_scalar(ev[0][:], spec[:, sl_r:sl_r + TTW],
                                            cfc[:, half:half + 1], None,
                                            ALU.mult)
                            v.tensor_scalar(ev[1][:], spec[:, sl_i:sl_i + TTW],
                                            cfs[:, half:half + 1], None,
                                            ALU.mult)
                            v.tensor_scalar(ev[2][:], spec[:, sl_r:sl_r + TTW],
                                            cfs[:, half:half + 1], None,
                                            ALU.mult)
                            v.tensor_scalar(ev[3][:], spec[:, sl_i:sl_i + TTW],
                                            cfc[:, half:half + 1], None,
                                            ALU.mult)
                            v.tensor_sub(spec[:, sl_r:sl_r + TTW], ev[0][:],
                                         ev[1][:])
                            v.tensor_add(spec[:, sl_i:sl_i + TTW], ev[2][:],
                                         ev[3][:])
                            dep = ctx.inc(v.drain(), sDV)
                            for j in (0, 1, 2, 3):
                                ctx.gsetw(EVN(j), dep)
                            ctx.mark(f"vpre_{half}_{tt}", sDV)
                @blk.vector
                def _(v):
                    ctx.inc(v.drain(), sDV)
                    ctx.mark("vpre_done", sDV)

                spec3 = spec[:].rearrange("p (c t) -> p c t", c=4)

                def scan_step(v, zprev_ap, vslot_ap, dst_ap):
                    v.tensor_tensor(cfA[:], zprev_ap, cfCD[:], ALU.mult)
                    v.tensor_tensor(cfB[:, 0:2], zprev_ap[:, 2:4],
                                    cfSD[:, 0:2], ALU.mult)
                    v.tensor_tensor(cfB[:, 2:4], zprev_ap[:, 0:2],
                                    cfSD[:, 2:4], ALU.mult)
                    v.drain()
                    v.tensor_add(cfA[:], cfA[:], cfB[:])
                    v.drain()
                    v.tensor_tensor(cfA[:], cfA[:], vslot_ap, ALU.add)
                    v.drain()
                    v.tensor_scalar(dst_ap, cfA[:], 10.0, -10.0, ALU.min,
                                    ALU.max)
                    v.drain()

                @blk.vector
                def _(v):
                    ctx.wait(v, "vpre_done")
                    for t in range(T):
                        scan_step(v, zstA[:], spec3[:, :, t], zstA[:])
                    ctx.inc(v.drain(), sDV)
                    ctx.mark("scanA_done", sDV)

                def emit_exchange_scanB():
                    @blk.vector
                    def _(v):
                        ctx.wait(v, "scanA_done")
                        for c in range(4):
                            v.tensor_scalar(zcc[:, ts(c, 8)],
                                            cm[:, 128 + c * 8:128 + c * 8 + 8],
                                            zstA[:, c:c + 1], None, ALU.mult)
                        ctx.inc(v.drain(), sDV)
                        ctx.mark("zcc_ready", sDV)
                    @blk.sync
                    def _(sy):
                        ctx.wait(sy, "zcc_ready")
                        ctx.dma(sy, sLD, ccz_i.ap(), zcc[:])
                        ctx.mark("zar_in", sLD)
                    @blk.gpsimd
                    def _(g):
                        ctx.wait(g, "zar_in")
                        ctx.inc(g.collective_compute(
                            "AllReduce", ALU.add,
                            replica_groups=[list(range(NCORES))],
                            ins=[ccz_i.ap().opt()],
                            outs=[ccz_o.ap().opt()]), sCC)
                        g.wait_ge(sCC, ctx.cnt["sCC"])
                        ctx.dma(g, sGD, zcc[:], ccz_o.ap())
                        ctx.mark("zar_out", sGD)
                    @blk.vector
                    def _(v):
                        ctx.wait(v, "zar_out")
                        v.tensor_mul(zcc[:], zcc[:], cm[:, 160:192])
                        v.tensor_reduce(zbnd[:],
                                        zcc[:].rearrange("p (c s) -> p c s",
                                                         s=8),
                                        AX.X, ALU.add)
                        ctx.inc(v.drain(), sDV)
                        ctx.mark("zbnd_ready", sDV)
                    @blk.vector
                    def _(v):
                        ctx.wait(v, "zbnd_ready")
                        for t in range(T):
                            zp = zbnd[:] if t == 0 else spec3[:, :, t - 1]
                            scan_step(v, zp, spec3[:, :, t], spec3[:, :, t])
                        ctx.inc(v.drain(), sDV)
                        ctx.mark("scanB_done", sDV)

                def emit_ffn(tag, rhs_hm, w1_ready, w2_src, b1_tile, b_ready,
                             combine, pre=None, mid=None):
                    for tt in range(TTN):
                        pm = pre(tt) if pre else None
                        @blk.sync
                        def _(sy, tt=tt, pm=pm):
                            if pm:
                                ctx.wait(sy, pm)
                            if tt == 0:
                                ctx.wait_total(
                                    sy, ctx.group_of(rhs_hm[0, 0]))
                            for k in range(KC):
                                ctx.gw(sy, f"rbf2{k}")
                                dep = ctx.dma(sy, sLD, rbf2[k][:],
                                              rhs_hm[k, tt])
                                ctx.gsetw(f"rbf2{k}", dep)
                            ctx.mark(tag + f"_rhs_{tt}", sLD)
                        for m in range(32):
                            bank = m % 4
                            @blk.tensor
                            def _(te, tt=tt, m=m, bank=bank):
                                if tt == 0 and m == 0:
                                    ctx.wait(te, w1_ready)
                                ctx.wait(te, tag + f"_rhs_{tt}")
                                ctx.gw(te, f"ps{bank}")
                                for kk in range(KC):
                                    inst = te.matmul(
                                        ps[bank][:, :],
                                        wA[kk][:, ts(m, 128)],
                                        rbf2[kk][:],
                                        start=(kk == 0), stop=(kk == KC - 1))
                                dep = ctx.inc(inst, sMM)
                                ctx.gsetw(f"ps{bank}", dep)
                                ctx.mark(tag + f"_p1_{tt}_{m}", sMM)
                            @blk.scalar
                            def _(sc, tt=tt, m=m, bank=bank):
                                if tt == 0 and m == 0 and b_ready:
                                    ctx.wait(sc, b_ready)
                                ctx.wait(sc, tag + f"_p1_{tt}_{m}")
                                ctx.gw(sc, f"evb{m}")
                                sc.activation(evb[m][:], ps[bank][:, :],
                                              AF.Gelu,
                                              bias=b1_tile[:, m:m + 1])
                                dep = ctx.inc(sc.drain(), sAC)
                                ctx.gsetw(f"evb{m}", dep)
                                ctx.gaddr(f"ps{bank}", dep)
                                ctx.mark(tag + f"_h_{tt}_{m}", sAC)
                        if tt == TTN - 1 and mid:
                            mid()
                        for dh in range(2):
                            @blk.gpsimd
                            def _(g, tt=tt, dh=dh):
                                base = ctx.cnt["sMM"]
                                for m in range(32):
                                    if m >= NW2:
                                        g.wait_ge(sMM, base + m - NW2 + 1)
                                    else:
                                        ctx.gw(g, f"w2s{m % NW2}")
                                    dep = ctx.dma(
                                        g, sGD, w2s[m % NW2][:],
                                        w2_src[ts(m, 128),
                                               dh * 512:dh * 512 + 512])
                                    ctx.gsetw(f"w2s{m % NW2}", dep)
                                    ctx.mark(tag + f"_w2_{tt}_{dh}_{m}", sGD)
                            @blk.tensor
                            def _(te, tt=tt, dh=dh):
                                if dh == 0:
                                    ctx.wait(te, tag + f"_h_{tt}_31")
                                for j in range(4):
                                    ctx.gw(te, f"ps{j}")
                                dep = None
                                for m in range(32):
                                    ctx.wait(te,
                                             tag + f"_w2_{tt}_{dh}_{m}")
                                    for j in range(4):
                                        inst = te.matmul(
                                            ps[j][:, :],
                                            w2s[m % NW2][:, ts(j, 128)],
                                            evb[m][:],
                                            start=(m == 0), stop=(m == 31))
                                    dep = ctx.inc(inst, sMM)
                                    ctx.gaddr(f"w2s{m % NW2}", dep)
                                for j in range(4):
                                    ctx.gsetw(f"ps{j}", dep)
                                if dh == 1:
                                    for m in range(32):
                                        ctx.gaddr(f"evb{m}", dep)
                                ctx.mark(tag + f"_p2_{tt}_{dh}", sMM)
                            for j in range(4):
                                combine(tt, dh * 4 + j, j)
                    ctx.marks[tag + "_done"] = [(sMM, ctx.cnt["sMM"])]

                def w1_cast(e):
                    @blk.gpsimd
                    def _(g, e=e):
                        if e == 0:
                            ctx.wait(g, f"wt_mm_{TTN - 1}_3")
                        else:
                            ctx.wait(g, f"E{e - 1}_p1_{TTN - 1}_31")
                        src = w_ffn1T if e == E else w1T[e]
                        for k in range(KC):
                            ctx.dma(g, sGD, wA[k][:, 0:4096],
                                    src[ts(k, 128), :])
                        ctx.mark(f"w1c_{e}", sGD)

                def be_load(e):
                    @blk.sync
                    def _(sy, e=e):
                        if e > 0:
                            ctx.wait(sy, f"acc_{e - 1}_{TTN - 1}_{KC - 1}")
                        ctx.dma(sy, sLD, b1e[:], b1_c[e])
                        ctx.dma(sy, sLD, b2e[:], b2_c[e])
                        ctx.dma(sy, sLD, onehe[:], onehot8[:, ts(e, 128)])
                        ctx.mark(f"be_{e}", sLD)

                def moe_pre_f(e):
                    def pre(tt, e=e):
                        @blk.sync
                        def _(sy, tt=tt, e=e):
                            if e == 0:
                                ctx.wait_total(sy, "wem_h")
                            else:
                                ctx.wait(sy, f"rt_sp_{tt}")
                            ctx.wait(sy, f"be_{e}")
                            ctx.dma(sy, sLD, rt8[0:8, :],
                                    wem_h[0:8, ts(tt, TTW)])
                            ctx.mark(f"wemld_{e}_{tt}", sLD)
                        @blk.tensor
                        def _(te, tt=tt, e=e):
                            ctx.wait(te, f"wemld_{e}_{tt}")
                            ctx.gw(te, "psB")
                            te.matmul(psB[:, :], onehe[:], rt8[0:8, :],
                                      start=True, stop=True)
                            dep = ctx.inc(te.drain(), sMM)
                            ctx.gsetw("psB", dep)
                            ctx.mark(f"wbcm_{e}_{tt}", sMM)
                        @blk.scalar
                        def _(sc, tt=tt, e=e):
                            ctx.wait(sc, f"wbcm_{e}_{tt}")
                            ctx.gw(sc, "wbc_t")
                            sc.activation(wbc_t[:], psB[:, :], AF.Identity)
                            dep = ctx.inc(sc.drain(), sAC)
                            ctx.gsetw("wbc_t", dep)
                            ctx.gaddr("psB", dep)
                            ctx.mark(f"wbc_{e}_{tt}", sAC)
                        if e > 0:
                            @blk.sync
                            def _(sy, tt=tt, e=e):
                                ctx.wait(sy, f"accsp_{e - 1}_{tt}_{KC - 1}")
                                for d in range(KC):
                                    ctx.gw(sy, EVN(d))
                                    dep = ctx.dma(sy, sLD, ev[d][:],
                                                  acc_h[d, tt])
                                    ctx.gsetw(EVN(d), dep)
                                ctx.mark(f"accld_{e}_{tt}", sLD)
                        return f"rt_sp_{tt}"
                    return pre

                def mk_moe_combine(e):
                    def combine(tt, d, j, e=e):
                        @blk.scalar
                        def _(sc, tt=tt, d=d, j=j, e=e):
                            if tt == 0 and d == 0:
                                ctx.wait(sc, f"be_{e}")
                            ctx.wait(sc, f"E{e}_p2_{tt}_{d // 4}")
                            ctx.gw(sc, f"ctp{j}")
                            sc.activation(ctp[j][:], ps[j][:, :], AF.Identity,
                                          bias=b2e[:, d:d + 1])
                            dep = ctx.inc(sc.drain(), sAC)
                            ctx.gsetw(f"ctp{j}", dep)
                            ctx.gaddr(f"ps{j}", dep)
                            ctx.mark(f"cta_{e}_{tt}_{d}", sAC)
                        @blk.gpsimd
                        def _(g, tt=tt, d=d, j=j, e=e):
                            ctx.wait(g, f"cta_{e}_{tt}_{d}")
                            if d == 0:
                                ctx.wait(g, f"wbc_{e}_{tt}")
                                if e > 0:
                                    ctx.wait(g, f"accld_{e}_{tt}")
                            ctx.gw(g, EVN(d))
                            if e == 0:
                                g.tensor_tensor(ev[d][:], ctp[j][:],
                                                wbc_t[:], ALU.mult)
                            else:
                                g.tensor_tensor(ctp[j][:], ctp[j][:],
                                                wbc_t[:], ALU.mult)
                                g.tensor_add(ev[d][:], ev[d][:], ctp[j][:])
                            dep = ctx.inc(g.drain(), sGP)
                            ctx.gsetw(EVN(d), dep)
                            ctx.gaddr(f"ctp{j}", dep)
                            ctx.gaddr("wbc_t", dep)
                            ctx.mark(f"acc_{e}_{tt}_{d}", sGP)
                        @blk.sync
                        def _(sy, tt=tt, d=d, e=e):
                            ctx.wait(sy, f"acc_{e}_{tt}_{d}")
                            dep = ctx.dma(sy, sLD, acc_h[d, tt], ev[d][:])
                            ctx.gaddr(EVN(d), dep)
                            ctx.mark(f"accsp_{e}_{tt}_{d}", sLD)
                    return combine

                w1_cast(0)
                for e in range(E):
                    be_load(e)
                    emit_ffn(f"E{e}", xn1bf_h, f"w1c_{e}", w2T[e], b1e,
                             f"be_{e}", mk_moe_combine(e),
                             pre=moe_pre_f(e),
                             mid=(lambda e=e: w1_cast(e + 1)))
                    if e == 3:
                        emit_exchange_scanB()

                if spo_h is not None:
                    @blk.sync
                    def _(sy):
                        ctx.wait(sy, "scanB_done")
                        ctx.dma(sy, sLD, spo_h, spec[:])
                        ctx.mark("spo_dbg", sLD)

                for d in range(KC):
                    @blk.sync
                    def _(sy, d=d):
                        for c in range(4):
                            ctx.gw(sy, f"wfF{c}")
                            dep = ctx.dma(sy, sLD, wfF[c][:],
                                          w_fromT[ts(c, 128),
                                                  d * 128:d * 128 + 128])
                            ctx.gsetw(f"wfF{c}", dep)
                        ctx.mark(f"wfld_{d}", sLD)
                    for tt in range(TTN):
                        bank = (d * TTN + tt) % 4
                        @blk.tensor
                        def _(te, tt=tt, d=d, bank=bank):
                            if tt == 0 and d == 0:
                                ctx.wait(te, "scanB_done")
                            ctx.wait(te, f"wfld_{d}")
                            ctx.gw(te, f"ps{bank}")
                            for c in range(4):
                                inst = te.matmul(
                                    ps[bank][:, :], wfF[c][:],
                                    spec[:, c * T + tt * TTW:
                                         c * T + tt * TTW + TTW],
                                    start=(c == 0), stop=(c == 3))
                            dep = ctx.inc(inst, sMM)
                            ctx.gsetw(f"ps{bank}", dep)
                            for c in range(4):
                                ctx.gaddr(f"wfF{c}", dep)
                            ctx.mark(f"wf_mm_{tt}_{d}", sMM)
                        @blk.scalar
                        def _(sc, tt=tt, d=d, bank=bank):
                            ctx.wait(sc, f"wf_mm_{tt}_{d}")
                            ctx.gw(sc, EVN(8))
                            sc.activation(ev[8][:], ps[bank][:, :],
                                          AF.Identity, scale=0.1)
                            dep = ctx.inc(sc.drain(), sAC)
                            ctx.gsetw(EVN(8), dep)
                            ctx.gaddr(f"ps{bank}", dep)
                            ctx.mark(f"wf_ev_{tt}_{d}", sAC)
                        @blk.sync
                        def _(sy, tt=tt, d=d):
                            if tt == 0 and d == 0:
                                ctx.wait_total(sy, "xn1_h")
                            ctx.gw(sy, EVN(9))
                            dep = ctx.dma(sy, sLD, ev[9][:], xn1_h[d, tt])
                            ctx.gsetw(EVN(9), dep)
                            ctx.mark(f"wf_x_{tt}_{d}", sLD)
                        @blk.vector
                        def _(v, tt=tt, d=d):
                            ctx.wait(v, f"wf_ev_{tt}_{d}")
                            ctx.wait(v, f"wf_x_{tt}_{d}")
                            v.tensor_add(ev[8][:], ev[8][:], ev[9][:])
                            dep = ctx.inc(v.drain(), sDV)
                            ctx.gsetw(EVN(8), dep)
                            ctx.gaddr(EVN(9), dep)
                            ctx.mark(f"x2_{tt}_{d}", sDV)
                        @blk.sync
                        def _(sy, tt=tt, d=d):
                            ctx.wait(sy, f"x2_{tt}_{d}")
                            dep = ctx.dma(sy, sLD, x2_h[d, tt], ev[8][:])
                            ctx.gaddr(EVN(8), dep)
                            ctx.mark(f"x2sp_{tt}_{d}", sLD)

                for tt in range(TTN):
                    def ln3_apply(k, tt=tt):
                        @blk.vector
                        def _(v, k=k, tt=tt):
                            ctx.wait(v, f"LN3{tt}_xc_{k}")
                            dj = k % 2
                            ctx.gw(v, EVN(dj))
                            ctx.gw(v, "wdq1")
                            v.tensor_mul(ev[dj][:], ev[4][:], ev[6][:])
                            v.tensor_scalar(wdq1[:], ev[dj][:],
                                            l3g[:, k:k + 1],
                                            l3b[:, k:k + 1], ALU.mult,
                                            ALU.add)
                            dep = ctx.inc(v.drain(), sDV)
                            ctx.gsetw(EVN(dj), dep)
                            ctx.gsetw("wdq1", dep)
                            ctx.gaddr(EVN(4), dep)
                            ctx.mark(f"ln3ap_{tt}_{k}", sDV)
                        @blk.sync
                        def _(sy, k=k, tt=tt):
                            ctx.wait(sy, f"ln3ap_{tt}_{k}")
                            dep = ctx.dma(sy, sLD, ln3bf_h[k, tt], wdq1[:])
                            ctx.gaddr("wdq1", dep)
                            ctx.mark(f"ln3sp_{tt}_{k}", sLD)

                    emit_ln_tt(f"LN3{tt}",
                               lambda k, tt=tt: x2_h[k, tt],
                               f"x2sp_{tt}_{KC - 1}", False, ln3_apply)

                def l1ffn_combine(tt, d, j):
                    @blk.scalar
                    def _(sc, tt=tt, d=d, j=j):
                        ctx.wait(sc, f"F1_p2_{tt}_{d // 4}")
                        ctx.gw(sc, EVN(8))
                        sc.activation(ev[8][:], ps[j][:, :], AF.Identity,
                                      scale=0.1, bias=bf2s[:, d:d + 1])
                        dep = ctx.inc(sc.drain(), sAC)
                        ctx.gsetw(EVN(8), dep)
                        ctx.gaddr(f"ps{j}", dep)
                        ctx.mark(f"s1e_{tt}_{d}", sAC)
                    @blk.sync
                    def _(sy, tt=tt, d=d):
                        if tt == 0 and d == 0:
                            ctx.wait_total(sy, "x2_h")
                        ctx.gw(sy, EVN(9))
                        dep = ctx.dma(sy, sLD, ev[9][:], x2_h[d, tt])
                        ctx.gsetw(EVN(9), dep)
                        ctx.mark(f"s1x_{tt}_{d}", sLD)
                    @blk.vector
                    def _(v, tt=tt, d=d):
                        ctx.wait(v, f"s1e_{tt}_{d}")
                        ctx.wait(v, f"s1x_{tt}_{d}")
                        v.tensor_add(ev[8][:], ev[8][:], ev[9][:])
                        dep = ctx.inc(v.drain(), sDV)
                        ctx.gsetw(EVN(8), dep)
                        ctx.gaddr(EVN(9), dep)
                        ctx.mark(f"s1_{tt}_{d}", sDV)
                    @blk.sync
                    def _(sy, tt=tt, d=d):
                        ctx.wait(sy, f"s1_{tt}_{d}")
                        dep = ctx.dma(sy, sLD, s1_h[d, tt], ev[8][:])
                        ctx.gaddr(EVN(8), dep)
                        ctx.mark(f"s1sp_{tt}_{d}", sLD)

                emit_ffn("F1", ln3bf_h, f"w1c_{E}", w_ffn2T, bf1, None,
                         l1ffn_combine,
                         pre=lambda tt: f"ln3sp_{tt}_{KC - 1}")

                for tt in range(TTN):
                    for k in range(KC):
                        @blk.sync
                        def _(sy, tt=tt, k=k):
                            ctx.wait(sy, f"accsp_{E - 1}_{tt}_{KC - 1}")
                            if tt == 0 and k == 0:
                                ctx.wait_total(sy, "x1_h")
                                ctx.wait_total(sy, "s1_h")
                            for evj, hsrc in ((0, acc_h), (1, x1_h),
                                              (2, s1_h)):
                                ctx.gw(sy, EVN(evj))
                                dep = ctx.dma(sy, sLD, ev[evj][:],
                                              hsrc[k, tt])
                                ctx.gsetw(EVN(evj), dep)
                            ctx.mark(f"fld_{tt}_{k}", sLD)
                        @blk.scalar
                        def _(sc, tt=tt, k=k):
                            ctx.wait(sc, f"fld_{tt}_{k}")
                            ctx.gw(sc, EVN(3))
                            sc.activation(ev[3][:], ev[0][:], AF.Silu)
                            dep = ctx.inc(sc.drain(), sAC)
                            ctx.gsetw(EVN(3), dep)
                            ctx.gaddr(EVN(0), dep)
                            ctx.mark(f"fsil_{tt}_{k}", sAC)
                        @blk.vector
                        def _(v, tt=tt, k=k):
                            ctx.wait(v, f"fsil_{tt}_{k}")
                            v.tensor_mul(ev[3][:], ev[3][:], ev[2][:])
                            v.tensor_add(ev[3][:], ev[3][:], ev[1][:])
                            dep = ctx.inc(v.drain(), sDV)
                            ctx.gsetw(EVN(3), dep)
                            ctx.gaddr(EVN(1), dep)
                            ctx.gaddr(EVN(2), dep)
                            ctx.mark(f"fout_{tt}_{k}", sDV)
                        @blk.sync
                        def _(sy, tt=tt, k=k):
                            ctx.wait(sy, f"fout_{tt}_{k}")
                            dep = ctx.dma(sy, sLD,
                                          out_fm[k][:, ts(tt, TTW)],
                                          ev[3][:])
                            ctx.gaddr(EVN(3), dep)
                            ctx.mark(f"osp_{tt}_{k}", sLD)
        else:
            @blk.vector
            def _(v):
                ctx.gw(v, EVN(0))
                v.memset(ev[0][:], 0.0)
                dep = ctx.inc(v.drain(), sDV)
                ctx.gsetw(EVN(0), dep)
                ctx.mark("zout", sDV)
            @blk.sync
            def _(sy):
                ctx.wait(sy, "zout")
                for k in range(KC):
                    for tt in range(TTN):
                        ctx.dma(sy, sLD, out_fm[k][:, ts(tt, TTW)], ev[0][:])
                ctx.mark("osp", sLD)

        final = [(s, ctx.cnt[s.name]) for s in ctx.sems.values()]
        @blk.gpsimd
        def _(g):
            for s, v in final:
                if v > 0:
                    g.wait_ge(s, v)

    return nc



_cache = {}


def _make_in_maps(inputs):
    x = np.asarray(inputs["L0_attn_x"], np.float32)
    base = {}
    base["w_qkvaT"] = np.ascontiguousarray(
        np.asarray(inputs["L0_attn_w_qkva"], np.float32).T)
    base["gamma_c"] = np.ascontiguousarray(
        np.asarray(inputs["L0_attn_gamma"], np.float32).reshape(KC, 128).T)
    base["wup_n"] = np.ascontiguousarray(
        np.asarray(inputs["L0_ffn_w_up"], np.float32))
    base["wgate_n"] = np.ascontiguousarray(
        np.asarray(inputs["L0_ffn_w_gate"], np.float32))
    base["wdown_n"] = np.ascontiguousarray(
        np.asarray(inputs["L0_ffn_w_down"], np.float32))
    base["w_toT"] = np.ascontiguousarray(
        np.asarray(inputs["L1_attn_W_to"], np.float32).T)
    base["w_fromT"] = np.ascontiguousarray(
        np.asarray(inputs["L1_attn_W_from"], np.float32).T)
    base["ldecay_c"] = np.ascontiguousarray(
        np.asarray(inputs["L1_attn_log_decay"], np.float32).reshape(2, 128).T)
    base["freq_c"] = np.ascontiguousarray(
        np.asarray(inputs["L1_attn_frequencies"], np.float32
                   ).reshape(2, 128).T)
    for nm, key in (("ln1g_c", "L1_attn_ln1_g"), ("ln1b_c", "L1_attn_ln1_b"),
                    ("ln3g_c", "L1_attn_ln3_g"), ("ln3b_c", "L1_attn_ln3_b"),
                    ("b_ffn2_c", "L1_attn_b_ffn2")):
        base[nm] = np.ascontiguousarray(
            np.asarray(inputs[key], np.float32).reshape(KC, 128).T)
    base["b_ffn1_c"] = np.ascontiguousarray(
        np.asarray(inputs["L1_attn_b_ffn1"], np.float32).reshape(32, 128).T)
    base["w_ffn1T"] = np.ascontiguousarray(
        np.asarray(inputs["L1_attn_W_ffn1"], np.float32).T)
    base["w_ffn2T"] = np.ascontiguousarray(
        np.asarray(inputs["L1_attn_W_ffn2"], np.float32).T)
    base["wrT"] = np.ascontiguousarray(
        np.asarray(inputs["L1_ffn_Wr"], np.float32).T)
    base["w1T"] = np.ascontiguousarray(
        np.transpose(np.asarray(inputs["L1_ffn_W1"], np.float32), (0, 2, 1)))
    base["w2T"] = np.ascontiguousarray(
        np.transpose(np.asarray(inputs["L1_ffn_W2"], np.float32), (0, 2, 1)))
    base["b1_c"] = np.ascontiguousarray(
        np.asarray(inputs["L1_ffn_b1"], np.float32
                   ).reshape(E, 32, 128).transpose(0, 2, 1))
    base["b2_c"] = np.ascontiguousarray(
        np.asarray(inputs["L1_ffn_b2"], np.float32
                   ).reshape(E, KC, 128).transpose(0, 2, 1))
    oh = np.zeros((8, 8, 128), np.float32)
    for e in range(8):
        oh[e, e, :] = 1.0
    base["onehot8"] = np.ascontiguousarray(
        oh.transpose(1, 0, 2).reshape(8, 1024))

    in_maps = []
    for c in range(NCORES):
        b, h = c >> 1, c & 1
        sl = x[b, h * T:(h + 1) * T, :]
        x_fm = np.ascontiguousarray(sl.T.reshape(KC, 128, T))
        cmask_a = np.zeros((128, 192), np.float32)
        my = np.zeros((128, 8, 8), np.float32)
        my[:, :, c] = 1.0
        cmask_a[:, 0:64] = my.reshape(128, 64)
        pm = np.zeros((128, 8, 8), np.float32)
        if h == 1:
            pm[:, :, c - 1] = 1.0
        cmask_a[:, 64:128] = pm.reshape(128, 64)
        mz = np.zeros((128, 4, 8), np.float32)
        mz[:, :, c] = 1.0
        cmask_a[:, 128:160] = mz.reshape(128, 32)
        pz = np.zeros((128, 4, 8), np.float32)
        if h == 1:
            pz[:, :, c - 1] = 1.0
        cmask_a[:, 160:192] = pz.reshape(128, 32)
        m = dict(base)
        m["x_fm"] = x_fm
        m["cmask"] = cmask_a
        in_maps.append(m)
    return in_maps


def kernel(**inputs):
    key = ("prog", STAGE, tuple(sorted(DEBUG)))
    if key not in _cache:
        _cache[key] = build_program(debug=DEBUG, stage=STAGE)
    nc = _cache[key]
    in_maps = _make_in_maps(inputs)
    res = run_bass_kernel_spmd(nc, in_maps, core_ids=list(range(NCORES)))
    out = np.empty((4, 4096, D), np.float32)
    for c in range(NCORES):
        b, h = c >> 1, c & 1
        o = np.asarray(res.results[c]["out_fm"]).reshape(D, T)
        out[b, h * T:(h + 1) * T, :] = o.T
    kernel.last_results = res
    return out
